# version 15
# baseline (speedup 1.0000x reference)
from contextlib import ExitStack

import numpy as np

import concourse.bass as bass
import concourse.mybir as mybir
from concourse.bass_utils import run_bass_kernel_spmd

F32 = mybir.dt.float32
BF16 = mybir.dt.bfloat16
U32 = mybir.dt.uint32
ALU = mybir.AluOpType
ACT = mybir.ActivationFunctionType
AX = mybir.AxisListType

B = 2
S = 8192
D = 1024
NCORE = 8
POWN = S // NCORE
NT1 = S // 128
NTO = 2 * POWN // 128
NQ = S // 512
NKT = S // 128
GSZ = 3
EPS_RMS = float(np.finfo(np.float32).eps)
EPS_LN = 1e-5

_CACHE = {}


class Sem:
    def __init__(self, nc, ctx, name):
        self.h = ctx.enter_context(nc.semaphore(name))
        self.n = 0
        self.name = name

    def inc(self, instr, k=1):
        instr.then_inc(self.h, k)
        self.n += k
        return (self, self.n)

    def tok(self):
        return (self, self.n)


class Waits:
    def __init__(self):
        self.seen = {}

    def w(self, eng, ename, *deps):
        for d in deps:
            if d is None:
                continue
            sem, val = d
            if val <= 0:
                continue
            key = (ename, sem.name)
            if self.seen.get(key, 0) < val:
                eng.wait_ge(sem.h, val)
                self.seen[key] = val


def p2_groups():
    groups = []
    for u in range(4):
        for qi in range(NQ):
            nkt = 4 * (qi + 1)
            kts = list(range(nkt))
            for i in range(0, nkt, GSZ):
                groups.append((u, qi, kts[i : i + GSZ]))
    eng_of = []
    load = {0: 0.0, 1: 0.0}
    for u, qi, kl in groups:
        nkt = 4 * (qi + 1)
        diag = any(kt >= nkt - 4 for kt in kl)
        w = len(kl) * 512
        cs_ = (172 + w) / 1.2
        cv_ = (120 + w / 2 + 151 + w) / 0.96 + ((58 + w / 2) / 0.96 if diag else 0)
        en = 1 if diag else (0 if load[0] + cs_ <= load[1] + cv_ else 1)
        eng_of.append(en)
        load[en] += cs_ if en == 0 else cv_
    return groups, eng_of


def build_nc():
    nc = bass.Bass("TRN2", target_bir_lowering=False)

    def inp(name, shape, dt=F32):
        return nc.dram_tensor(name, shape, dt, kind="ExternalInput")

    xfm = inp("xfm", [D, S])
    x0 = inp("x0", [2, POWN, D])
    wqkvT = inp("wqkvT", [D, 768])
    bqkvr = inp("bqkvr", [1, 768])
    ropeq = inp("ropeq", [S, 128])
    ropek = inp("ropek", [S, 128])
    normw = inp("normw", [128, 512])
    woT = inp("woT", [D, D])
    masksi = inp("masksi", [128, 4, 512])
    identi = inp("identi", [128, 128])
    winT = inp("winT", [D, 3072])
    binpr = inp("binpr", [1, 3072])
    woutT = inp("woutT", [D, D])
    boutr = inp("boutr", [1, D])
    ln1g = inp("ln1g", [128, D])
    ln1b = inp("ln1b", [128, D])
    ln2g = inp("ln2g", [128, D])
    ln2b = inp("ln2b", [128, D])
    routT = inp("routT", [D, 8])
    rbr = inp("rbr", [1, 8])
    iotam = inp("iotam", [128, 8])
    w1T = inp("w1T", [8, D, D])
    b1c = inp("b1c", [8, 128, 8])
    w2T = inp("w2T", [8, D, D])
    b2r = inp("b2r", [8, D])
    zoffi = inp("zoffi", [1, 1], U32)

    out = nc.dram_tensor("out", [2, POWN, D], F32, kind="ExternalOutput")

    z_send1 = nc.dram_tensor("z_send1", [4, 2, 64, 2048], BF16)
    z_send2 = nc.dram_tensor("z_send2", [4, 2, 64, 2048], BF16)
    z_all1 = nc.dram_tensor("z_all1", [4, NCORE, 2, 64, 2048], BF16)
    z_all2 = nc.dram_tensor("z_all2", [4, NCORE, 2, 64, 2048], BF16)
    x1_dram = nc.dram_tensor("x1_dram", [NTO, 128, D], F32)
    h_dram = nc.dram_tensor("h_dram", [NTO, 128, D], F32)
    x1b_dram = nc.dram_tensor("x1b_dram", [NTO, 128, D], BF16)
    hb_dram = nc.dram_tensor("hb_dram", [NTO, 128, D], BF16)

    groups, eng_of = p2_groups()
    NG = len(groups)
    sq_done = [None] * NG
    pat_done = [None] * NG
    cnt = {0: 0, 1: 0}
    for gi, (u, qi, kl) in enumerate(groups):
        en = eng_of[gi]
        nkt = 4 * (qi + 1)
        diag = any(kt >= nkt - 4 for kt in kl)
        cnt[en] += 1
        sq_done[gi] = (en, cnt[en])
        if en == 1:
            cnt[en] += 1
        if diag:
            cnt[en] += 1
        pat_done[gi] = (en, cnt[en])

    W = Waits()

    with ExitStack() as ctx:
        e = ctx.enter_context

        tt = Sem(nc, ctx, "tt")
        ts = Sem(nc, ctx, "ts")
        tz = Sem(nc, ctx, "tz")
        ss = Sem(nc, ctx, "ss")
        vv = Sem(nc, ctx, "vv")
        ze = Sem(nc, ctx, "ze")
        qe = Sem(nc, ctx, "qe")
        gg = Sem(nc, ctx, "gg")
        ff = Sem(nc, ctx, "ff")
        cs = Sem(nc, ctx, "cs")
        dwA = Sem(nc, ctx, "dwA")
        dwB = Sem(nc, ctx, "dwB")
        dwC = Sem(nc, ctx, "dwC")
        dwD = Sem(nc, ctx, "dwD")
        dc1 = Sem(nc, ctx, "dc1")
        dc4 = Sem(nc, ctx, "dc4")
        dc6 = Sem(nc, ctx, "dc6")
        dzs = Sem(nc, ctx, "dzs")
        dxs = [Sem(nc, ctx, f"dxs{i}") for i in range(2)]
        dro = [Sem(nc, ctx, f"dro{i}") for i in range(2)]
        dzi = [Sem(nc, ctx, f"dzi{i}") for i in range(2)]
        dx0 = [Sem(nc, ctx, f"dx0{i}") for i in range(2)]
        dsp = [Sem(nc, ctx, f"dsp{i}") for i in range(2)]
        dxp = [Sem(nc, ctx, f"dxp{i}") for i in range(2)]
        dhs = [Sem(nc, ctx, f"dhs{i}") for i in range(2)]
        dwe = [Sem(nc, ctx, f"dwe{i}") for i in range(2)]
        dbe = [Sem(nc, ctx, f"dbe{i}") for i in range(2)]
        dp6 = [Sem(nc, ctx, f"dp6{i}") for i in range(2)]
        dxb = Sem(nc, ctx, "dxb")
        dhb = Sem(nc, ctx, "dhb")
        dxf = [Sem(nc, ctx, f"dxf{i}") for i in range(2)]
        dhf = [Sem(nc, ctx, f"dhf{i}") for i in range(2)]
        dou = [Sem(nc, ctx, f"dou{i}") for i in range(2)]

        ones1 = e(nc.sbuf_tensor("ones1", [1, 128], BF16))
        epsr = e(nc.sbuf_tensor("epsr", [128, 1], F32))
        epsl = e(nc.sbuf_tensor("epsl", [128, 1], F32))
        idn = e(nc.sbuf_tensor("idn", [128, 128], BF16))
        idn32 = e(nc.sbuf_tensor("idn32", [128, 128], F32))
        zoff_sb = e(nc.sbuf_tensor("zoff_sb", [1, 1], U32))
        iot = e(nc.sbuf_tensor("iot", [128, 8], F32))
        block = e(nc.Block())

        @block.gpsimd
        def _(g):
            g.memset(ones1[:, :], 1.0)
            g.memset(epsr[:, :], EPS_RMS)
            g.memset(epsl[:, :], EPS_LN)
            dwA.inc(g.dma_start(out=idn[:, :], in_=identi[:, :]), 16)

        @block.sync
        def _(sy):
            dc1.inc(sy.dma_start(out=idn32[:, :], in_=identi[:, :]), 16)
            dc1.inc(sy.dma_start(out=zoff_sb[:, :], in_=zoffi[:, :]), 16)
            dc1.inc(sy.dma_start(out=iot[:, :], in_=iotam[:, :]), 16)


        def transpose8(src_fn, dst_fn, tp, idmat, dep_toks, prev_evac):
            mm_toks = [None, None]

            @block.tensor
            def _(te):
                W.w(te, "t", *dep_toks)
                if prev_evac[0] is not None:
                    W.w(te, "t", prev_evac[0])
                for kc in range(8):
                    tr = te.transpose(
                        tp[:, kc // 4, (kc % 4) * 128 : (kc % 4 + 1) * 128],
                        src_fn(kc),
                        idmat[:, :],
                    )
                    if kc % 4 == 3:
                        mm_toks[kc // 4] = tt.inc(tr)

            @block.scalar
            def _(s):
                for half in range(2):
                    W.w(s, "s", mm_toks[half])
                    for kc in range(4 * half, 4 * half + 4):
                        cp = s.copy(
                            dst_fn(kc),
                            tp[:, kc // 4, (kc % 4) * 128 : (kc % 4 + 1) * 128],
                        )
                prev_evac[0] = ss.inc(cp)

            return prev_evac[0]

        def emit_ln(Lsrc, dest, stats, pb, dep_tok, gains, extra_v):

            @block.vector
            def _(v):
                W.w(v, "v", dep_tok)
                c0 = vv.inc(v.tensor_tensor(lnj1[:, :], Lsrc, Lsrc, ALU.mult))
                W.w(v, "v", c0)
                vv.inc(v.reduce_sum(stats[:, pb, 1:2], lnj1[:, :], axis=AX.X))
                vv.inc(v.reduce_sum(stats[:, pb, 0:1], Lsrc, axis=AX.X))
                W.w(v, "v", vv.tok())
                m1 = vv.inc(
                    v.tensor_scalar(
                        stats[:, pb, 2:4], stats[:, pb, 0:2], 1.0 / D, None, ALU.mult
                    )
                )
                W.w(v, "v", m1)
                m2 = vv.inc(
                    v.tensor_tensor(
                        stats[:, pb, 0:1], stats[:, pb, 2:3], stats[:, pb, 2:3], ALU.mult
                    )
                )
                W.w(v, "v", m2)
                vv.inc(
                    v.scalar_tensor_tensor(
                        stats[:, pb, 1:2], stats[:, pb, 3:4], EPS_LN,
                        stats[:, pb, 0:1], ALU.add, ALU.subtract,
                    )
                )

            var_tok = vv.tok()

            @block.scalar
            def _(s):
                W.w(s, "s", var_tok)
                ss.inc(
                    s.activation(stats[:, pb, 1:2], stats[:, pb, 1:2], ACT.Sqrt)
                )

            sd_tok = ss.tok()

            fin = [None]

            @block.vector
            def _(v):
                W.w(v, "v", sd_tok)
                if extra_v is not None:
                    W.w(v, "v", *extra_v)
                i1 = vv.inc(v.reciprocal(stats[:, pb, 1:2], stats[:, pb, 1:2]))
                W.w(v, "v", i1)
                i2 = vv.inc(
                    v.tensor_tensor(
                        stats[:, pb, 0:1], stats[:, pb, 2:3], stats[:, pb, 1:2], ALU.mult
                    )
                )
                W.w(v, "v", i2)
                ap = vv.inc(
                    v.tensor_scalar(
                        dest, Lsrc, stats[:, pb, 1:2], stats[:, pb, 0:1],
                        ALU.mult, ALU.subtract,
                    )
                )
                if gains is not None:
                    gsb, bsb = gains
                    W.w(v, "v", ap)
                    g1 = vv.inc(v.tensor_tensor(dest, dest, gsb[:, :], ALU.mult))
                    W.w(v, "v", g1)
                    ap = vv.inc(v.tensor_tensor(dest, dest, bsb[:, :], ALU.add))
                fin[0] = ap

            return fin[0]

        with ExitStack() as ca:
            ea = ca.enter_context
            qf = ea(nc.sbuf_tensor("qf", [128, 2, S], BF16))
            kf = ea(nc.sbuf_tensor("kf", [128, 2, S], BF16))
            vt = ea(nc.sbuf_tensor("vt", [128, 4, NKT, 64], BF16))
            zsb = ea(nc.sbuf_tensor("zsb", [128, 2, S], BF16))
            xs = ea(nc.sbuf_tensor("xs", [128, 2, 8, 512], BF16))
            wqkv_sb = ea(nc.sbuf_tensor("wqkv_sb", [128, 8, 768], BF16))
            bqkv_sb = ea(nc.sbuf_tensor("bqkv_sb", [1, 768], BF16))
            ropq_sb = ea(nc.sbuf_tensor("ropq_sb", [128, 2, 4, 128], F32))
            ropk_sb = ea(nc.sbuf_tensor("ropk_sb", [128, 2, 4, 128], F32))
            normw_sb = ea(nc.sbuf_tensor("normw_sb", [128, 512], F32))
            qkvt = ea(nc.sbuf_tensor("qkvt", [128, 2, 768], F32))
            rqk = ea(nc.sbuf_tensor("rqk", [128, 4, 512], BF16))
            sqb = ea(nc.sbuf_tensor("sqb", [128, 512], F32))
            rscr = ea(nc.sbuf_tensor("rscr", [128, 512], F32))
            ssum = ea(nc.sbuf_tensor("ssum", [128, 2, 8], F32))
            rsb = ea(nc.sbuf_tensor("rsb", [128, 2, 8], F32))
            qn = ea(nc.sbuf_tensor("qn", [128, 512], F32))
            masks = ea(nc.sbuf_tensor("masks", [128, 4, 512], BF16))
            pat = ea(nc.sbuf_tensor("pat", [128, 6, 2, 512], BF16))
            sqscr = ea(nc.sbuf_tensor("sqscr", [128, 1024], BF16))

            ca_ps = ca.enter_context(ExitStack())
            qkv_ps = ca_ps.enter_context(nc.psum_tensor("qkv_ps", [128, 2, 1024], F32))
            tp_ps = ca_ps.enter_context(nc.psum_tensor("tp_ps", [128, 2, 1024], BF16))

            @block.gpsimd
            def _(g):
                dwA.inc(
                    g.dma_start(
                        out=wqkv_sb[:, :, :],
                        in_=wqkvT.ap().rearrange("(kc p) n -> p kc n", p=128),
                    ),
                    16,
                )
                dwA.inc(g.dma_start(out=bqkv_sb[:, :], in_=bqkvr[:, :]), 16)
                dwA.inc(g.dma_start(out=masks[:, :, :], in_=masksi[:, :, :]), 16)

            @block.sync
            def _(sy):
                dc1.inc(sy.dma_start(out=normw_sb[:, :], in_=normw[:, :]), 16)

            wq_tok = dwA.tok()
            consts_g = wq_tok
            nw_tok = dc1.tok()

            xs_tok = [None] * 16
            rope_tok = [None] * 16
            mm_tok = [None] * NT1
            evac_tok = [None] * NT1
            red_tok = [None] * NT1
            sd_tok = [None] * NT1
            rope_done = [None] * NT1
            vcp_tok = [None] * NT1
            tpmm_tok = [None] * NT1
            tp_tok = [None] * NT1

            def p1_transposes(tq):
                @block.tensor
                def _(te, tq=tq):
                    W.w(te, "t", rope_done[tq], consts_g)
                    if tq >= 2:
                        W.w(te, "t", tp_tok[tq - 2])
                    for hp in range(4):
                        tr = te.transpose(
                            tp_ps[:, tq % 2, hp * 128 : (hp + 1) * 128],
                            rqk[:, tq % 4, hp * 128 : (hp + 1) * 128],
                            idn[:, :],
                        )
                    tpmm_tok[tq] = tt.inc(tr)

                @block.scalar
                def _(s, tq=tq):
                    W.w(s, "s", tpmm_tok[tq])
                    for hp in range(2):
                        s.copy(
                            qf[:, hp, tq * 128 : (tq + 1) * 128],
                            tp_ps[:, tq % 2, hp * 128 : (hp + 1) * 128],
                        )
                        cp = s.copy(
                            kf[:, hp, tq * 128 : (tq + 1) * 128],
                            tp_ps[:, tq % 2, (2 + hp) * 128 : (3 + hp) * 128],
                        )
                    tp_tok[tq] = ss.inc(cp)

            for t in range(NT1):
                sl, tl = t // 4, t % 4

                if tl == 0:

                    @block.gpsimd
                    def _(g, sl=sl):
                        W.w(g, "g", consts_g)
                        if sl >= 2:
                            W.w(g, "g", mm_tok[4 * (sl - 2) + 3])
                        xs_tok[sl] = dxs[sl % 2].inc(
                            g.dma_start(
                                out=xs[:, sl % 2, :, :],
                                in_=xfm[:, sl * 512 : (sl + 1) * 512].rearrange(
                                    "(kc p) j -> p kc j", p=128
                                ),
                            ),
                            16,
                        )

                    @block.sync
                    def _(sy, sl=sl):
                        if sl >= 2:
                            W.w(sy, "y", rope_done[4 * (sl - 2) + 3])
                        dro[sl % 2].inc(
                            sy.dma_start(
                                out=ropq_sb[:, sl % 2, :, :],
                                in_=ropeq[sl * 512 : (sl + 1) * 512, :].rearrange(
                                    "(t p) c -> p t c", p=128
                                ),
                            ),
                            16,
                        )
                        rope_tok[sl] = dro[sl % 2].inc(
                            sy.dma_start(
                                out=ropk_sb[:, sl % 2, :, :],
                                in_=ropek[sl * 512 : (sl + 1) * 512, :].rearrange(
                                    "(t p) c -> p t c", p=128
                                ),
                            ),
                            16,
                        )

                @block.tensor
                def _(te, t=t, sl=sl, tl=tl):
                    W.w(te, "t", xs_tok[sl], wq_tok)
                    if t >= 2:
                        W.w(te, "t", evac_tok[t - 2])
                    pb = t % 2
                    for kc in range(8):
                        te.matmul(
                            qkv_ps[:, pb, 0:512],
                            xs[:, sl % 2, kc, tl * 128 : (tl + 1) * 128],
                            wqkv_sb[:, kc, 0:512],
                            start=(kc == 0),
                            stop=False,
                        )
                        te.matmul(
                            qkv_ps[:, pb, 512:768],
                            xs[:, sl % 2, kc, tl * 128 : (tl + 1) * 128],
                            wqkv_sb[:, kc, 512:768],
                            start=(kc == 0),
                            stop=False,
                        )
                    te.matmul(
                        qkv_ps[:, pb, 0:512], ones1[:, :], bqkv_sb[:, 0:512],
                        start=False, stop=True,
                    )
                    mm_tok[t] = tt.inc(
                        te.matmul(
                            qkv_ps[:, pb, 512:768], ones1[:, :], bqkv_sb[:, 512:768],
                            start=False, stop=True,
                        )
                    )

                @block.scalar
                def _(s, t=t):
                    W.w(s, "s", mm_tok[t])
                    if t >= 2:
                        W.w(s, "s", rope_done[t - 2], vcp_tok[t - 2])
                    evac_tok[t] = ss.inc(
                        s.copy(qkvt[:, t % 2, :], qkv_ps[:, t % 2, 0:768])
                    )

                @block.vector
                def _(v, t=t):
                    W.w(v, "v", evac_tok[t])
                    if t >= 1:
                        W.w(v, "v", red_tok[t - 1])
                    sqv = vv.inc(
                        v.tensor_tensor(
                            sqb[:, :], qkvt[:, t % 2, 0:512], qkvt[:, t % 2, 0:512],
                            ALU.mult,
                        )
                    )
                    W.w(v, "v", sqv)
                    rs1 = vv.inc(
                        v.reduce_sum(
                            ssum[:, t % 2, :].unsqueeze(2),
                            sqb[:, :].rearrange("p (g d) -> p g d", d=64),
                            axis=AX.X,
                        )
                    )
                    W.w(v, "v", rs1)
                    red_tok[t] = vv.inc(
                        v.tensor_scalar(
                            ssum[:, t % 2, :], ssum[:, t % 2, :], 1.0 / 64, EPS_RMS,
                            ALU.mult, ALU.add,
                        )
                    )

                if t % 2 == 0:
                    continue

                @block.scalar
                def _(s, t=t):
                    for u in (t - 1, t):
                        W.w(s, "s", red_tok[u])
                        sd_tok[u] = ss.inc(
                            s.activation(rsb[:, u % 2, :], ssum[:, u % 2, :], ACT.Sqrt)
                        )

                for u in (t - 1, t):
                    sl, tl = u // 4, u % 4

                    @block.vector
                    def _(v, t=u, sl=sl, tl=tl):
                        W.w(v, "v", sd_tok[t], nw_tok)
                        r1 = vv.inc(v.reciprocal(rsb[:, t % 2, :], rsb[:, t % 2, :]))
                        W.w(v, "v", r1)
                        r2 = vv.inc(
                            v.tensor_tensor(
                                qn[:, :].rearrange("p (g d) -> p g d", d=64),
                                qkvt[:, t % 2, 0:512].rearrange("p (g d) -> p g d", d=64),
                                rsb[:, t % 2, :].unsqueeze(2).broadcast_to([128, 8, 64]),
                                ALU.mult,
                            )
                        )
                        W.w(v, "v", r2)
                        r3 = vv.inc(
                            v.tensor_tensor(qn[:, :], qn[:, :], normw_sb[:, :], ALU.mult)
                        )
                        W.w(v, "v", r3, rope_tok[sl])
                        if t >= 4 and tpmm_tok[t - 4] is not None:
                            W.w(v, "v", tpmm_tok[t - 4])
                        qv = qn[:, :].rearrange("p (g d) -> p g d", d=64)
                        ov = rqk[:, t % 4, :].rearrange("p (g d) -> p g d", d=64)
                        rv = rscr[:, :].rearrange("p (g d) -> p g d", d=32)
                        for part, rp in ((0, ropq_sb), (1, ropk_sb)):
                            sg = slice(4 * part, 4 * part + 4)
                            cE = rp[:, sl % 2, tl, 0:32].unsqueeze(1).broadcast_to(
                                [128, 4, 32]
                            )
                            sE = rp[:, sl % 2, tl, 32:64].unsqueeze(1).broadcast_to(
                                [128, 4, 32]
                            )
                            cO = rp[:, sl % 2, tl, 64:96].unsqueeze(1).broadcast_to(
                                [128, 4, 32]
                            )
                            sO = rp[:, sl % 2, tl, 96:128].unsqueeze(1).broadcast_to(
                                [128, 4, 32]
                            )
                            qe_ = qv[:, sg, 0:32]
                            qo_ = qv[:, sg, 32:64]
                            v.tensor_tensor(rv[:, 0:4, :], qe_, cE, ALU.mult)
                            a2 = vv.inc(v.tensor_tensor(rv[:, 4:8, :], qo_, sE, ALU.mult))
                            W.w(v, "v", a2)
                            vv.inc(
                                v.tensor_tensor(
                                    ov[:, sg, 0:32], rv[:, 0:4, :], rv[:, 4:8, :],
                                    ALU.subtract,
                                )
                            )
                            v.tensor_tensor(rv[:, 8:12, :], qo_, cO, ALU.mult)
                            a4 = vv.inc(v.tensor_tensor(rv[:, 12:16, :], qe_, sO, ALU.mult))
                            W.w(v, "v", a4)
                            rope_done[t] = vv.inc(
                                v.tensor_tensor(
                                    ov[:, sg, 32:64], rv[:, 8:12, :], rv[:, 12:16, :],
                                    ALU.add,
                                )
                            )
                        W.w(v, "v", evac_tok[t])
                        vcp_tok[t] = vv.inc(
                            v.tensor_copy(
                                vt[:, :, t, :],
                                qkvt[:, t % 2, 512:768].rearrange("p (u d) -> p u d", d=64),
                            )
                        )

                if t >= 3:
                    p1_transposes(t - 3)
                    p1_transposes(t - 2)

            p1_transposes(NT1 - 2)
            p1_transposes(NT1 - 1)

            p1_s = ss.n
            p1_v = vv.n

            ca_ps.close()
            z_ps = ea(nc.psum_tensor("z_ps", [128, 2, 512], F32))
            sc_ps = ea(nc.psum_tensor("sc_ps", [128, 3, 2, 512], F32))

            p2sched = [(qi, kt) for qi in range(NQ) for kt in range(4 * qi + 4)]
            NI = len(p2sched)
            sq_eng = []
            load2 = {0: 0.0, 1: 0.0}
            for hb in range(2):
                for qi, kt in p2sched:
                    diag = kt >= 4 * qi
                    cs_ = (172 + 1024) / 1.2
                    cv_ = ((120 + 1024 if diag else 120 + 512) + 58 + 512) / 0.96
                    en = 1 if diag else (0 if load2[0] + cs_ <= load2[1] + cv_ else 1)
                    sq_eng.append(en)
                    load2[en] += cs_ if en == 0 else cv_
            sqrel = [None] * (2 * NI)
            patrdy = [None] * (2 * NI)
            cnt2 = {0: 0, 1: 0}
            for g in range(2 * NI):
                qi, kt = p2sched[g % NI]
                en = sq_eng[g]
                cnt2[en] += 1
                sqrel[g] = (en, cnt2[en])
                if en == 1:
                    cnt2[en] += 1
                patrdy[g] = (en, cnt2[en])

            def p2tok(entry):
                en, n = entry
                return ((ss if en == 0 else vv), (p1_s if en == 0 else p1_v) + n)

            zev_tok = [None] * 32

            def emit_z2(te, g):
                qip, ktp = p2sched[g % NI]
                hb = g // NI
                W.w(te, "t", p2tok(patrdy[g]))
                if ktp == 0:
                    idx = hb * NQ + qip
                    if idx >= 1:
                        W.w(te, "t", (ze, idx))
                last = ktp == 4 * qip + 3
                for ub in range(2):
                    mm = te.matmul(
                        z_ps[ub * 64 : ub * 64 + 64, ub, :],
                        vt[:, 2 * hb + ub, ktp, :],
                        pat[:, g % 6, ub, :],
                        start=(ktp == 0),
                        stop=last,
                        tile_position=(0, ub * 64),
                    )
                if last:
                    tz.inc(mm)

            @block.tensor
            def _(te):
                W.w(te, "t", (ss, p1_s), (vv, p1_v))
                for g in range(2 * NI):
                    qi, kt = p2sched[g % NI]
                    hb = g // NI
                    slot = g % 3
                    if g >= 3:
                        W.w(te, "t", p2tok(sqrel[g - 3]))
                    for ub in range(2):
                        p0 = ub * 64
                        mm = te.matmul(
                            sc_ps[:, slot, ub, :],
                            kf[p0 : p0 + 64, hb, kt * 128 : (kt + 1) * 128],
                            qf[p0 : p0 + 64, hb, qi * 512 : (qi + 1) * 512],
                            start=True,
                            stop=True,
                        )
                    ts.inc(mm)
                    if g >= 5:
                        emit_z2(te, g - 5)
                for gtail in range(2 * NI - 5, 2 * NI):
                    emit_z2(te, gtail)

            def emit_squares(eng, ename, engine_id, my):
                for g in range(2 * NI + 4):
                    if engine_id == 0 and g >= 4:
                        gd = g - 4
                        qi_d, kt_d = p2sched[gd % NI]
                        hb_d = gd // NI
                        if kt_d == 4 * qi_d + 3:
                            idx = hb_d * NQ + qi_d
                            W.w(eng, ename, (tz, idx + 1))
                            eng.copy(
                                zsb[0:64, hb_d, qi_d * 512 : (qi_d + 1) * 512],
                                z_ps[0:64, 0, :],
                            )
                            zev_tok[idx] = ze.inc(
                                eng.copy(
                                    zsb[64:128, hb_d, qi_d * 512 : (qi_d + 1) * 512],
                                    z_ps[64:128, 1, :],
                                )
                            )
                    if g >= 2 * NI:
                        continue
                    qi, kt = p2sched[g % NI]
                    hb = g // NI
                    if sq_eng[g] != engine_id:
                        continue
                    W.w(eng, ename, (ts, g + 1))
                    slot = g % 3
                    if engine_id == 0:
                        my.inc(
                            eng.activation(
                                pat[:, g % 6, :, :], sc_ps[:, slot, :, :], ACT.Square
                            )
                        )
                    else:
                        W.w(eng, ename, my.tok())
                        if kt >= 4 * qi:
                            m0 = kt - 4 * qi
                            c1 = my.inc(
                                eng.tensor_tensor(
                                    sqscr[:, :].rearrange("p (u j) -> p u j", u=2),
                                    sc_ps[:, slot, :, :],
                                    masks[:, m0, :].unsqueeze(1).broadcast_to(
                                        [128, 2, 512]
                                    ),
                                    ALU.mult,
                                )
                            )
                        else:
                            c1 = my.inc(
                                eng.tensor_scalar(
                                    sqscr[:, :], sc_ps[:, slot, :, :], 1.0, None,
                                    ALU.mult,
                                )
                            )
                        W.w(eng, ename, c1)
                        my.inc(
                            eng.tensor_tensor(
                                pat[:, g % 6, :, :],
                                sqscr[:, :].rearrange("p (u j) -> p u j", u=2),
                                sqscr[:, :].rearrange("p (u j) -> p u j", u=2),
                                ALU.mult,
                            )
                        )

            @block.scalar
            def _(s):
                emit_squares(s, "s", 0, ss)

            @block.vector
            def _(v):
                emit_squares(v, "v", 1, vv)

            @block.gpsimd
            def _(g):
                for half in range(2):
                    zs_, za_ = (z_send1, z_all1) if half == 0 else (z_send2, z_all2)
                    for j in range(4):
                        W.w(g, "g", (ze, 16 * half + 4 * (j + 1)))
                        dzs.inc(
                            g.dma_start(
                                out=zs_[j, :, :, :].rearrange("u d s -> (u d) s"),
                                in_=zsb[:, half, 2048 * j : 2048 * (j + 1)],
                            ),
                            16,
                        )
                        W.w(g, "g", dzs.tok())
                        cs.inc(
                            g.collective_compute(
                                "AllGather",
                                ALU.bypass,
                                replica_groups=[list(range(NCORE))],
                                ins=[zs_[j, :, :, :].opt()],
                                outs=[za_[j, :, :, :, :].opt()],
                            )
                        )

        cc_tok = cs.tok()

        cdE = ctx.enter_context(ExitStack())
        hf = cdE.enter_context(nc.sbuf_tensor("hf", [128, 8, 2 * POWN], BF16))
        assign = cdE.enter_context(nc.sbuf_tensor("assign", [128, 16], F32))
        routT_sb = cdE.enter_context(nc.sbuf_tensor("routT_sb", [128, 8, 8], BF16))
        rb_sb = cdE.enter_context(nc.sbuf_tensor("rb_sb", [1, 8], BF16))
        lgsb = cdE.enter_context(nc.sbuf_tensor("lgsb", [128, 2, 8], F32))
        r8 = cdE.enter_context(nc.sbuf_tensor("r8", [128, 3, 8], F32))
        rst = cdE.enter_context(nc.sbuf_tensor("rst", [128, 6], F32))
        rt_ps = cdE.enter_context(nc.psum_tensor("rt_ps", [128, 512], F32))
        lnj1b_t = cdE.enter_context(nc.sbuf_tensor("lnj1b", [128, D], F32))
        lnj2b_t = cdE.enter_context(nc.sbuf_tensor("lnj2b", [128, D], F32))

        cbE = ExitStack()
        winT_sb = cbE.enter_context(nc.sbuf_tensor("winT_sb", [128, 8, 3072], BF16))
        binp_sb = cbE.enter_context(nc.sbuf_tensor("binp_sb", [1, 3072], BF16))
        x1fp = cbE.enter_context(nc.sbuf_tensor("x1fp", [128, 2, 8, 2, 128], BF16))
        qkvp = cbE.enter_context(nc.sbuf_tensor("qkvp", [128, 2, 2, 3072], BF16))
        qkvf_ps = cbE.enter_context(nc.psum_tensor("qkvf_ps", [128, 3, 512], F32))
        wch_tok = [None] * 6

        with ExitStack() as cb:
            eb = cb.enter_context
            zin = eb(nc.sbuf_tensor("zin", [128, 2, 2, 4, 512], BF16))
            lnj1 = eb(nc.sbuf_tensor("lnj1a", [128, D], F32))
            lnj2 = eb(nc.sbuf_tensor("lnj2a", [128, D], F32))
            woT_sb = eb(nc.sbuf_tensor("woT_sb", [128, 8, D], BF16))
            x0t = eb(nc.sbuf_tensor("x0t", [128, 2, D], F32))
            Lb = eb(nc.sbuf_tensor("Lb", [128, 2, D], F32))
            x1t = eb(nc.sbuf_tensor("x1t", [128, 2, D], F32))
            stats = eb(nc.sbuf_tensor("stats", [128, 2, 4], F32))
            ao_ps = eb(nc.psum_tensor("ao_ps", [128, 2, 1024], F32))

            zin_tok = [None] * 4
            x0_tok = [None] * NTO
            ao_tok = [None] * NTO
            L_tok = [None] * NTO
            x1_tok = [None] * NTO
            sp_tok = [None] * NTO

            @block.gpsimd
            def _(g):
                W.w(g, "g", (ze, 32), (dzs, 128))
                dwB.inc(
                    g.dma_start(
                        out=woT_sb[:, :, :],
                        in_=woT.ap().rearrange("(kc p) n -> p kc n", p=128),
                    ),
                    16,
                )

            wo_tok = dwB.tok()
            zr = cb.enter_context(nc.gpsimd.register("zr"))
            zb_r = cb.enter_context(nc.gpsimd.register("zb_r"))

            def p3_tail(t):
                x1_tok[t] = emit_ln(
                    Lb[:, t % 2, :], x1t[:, t % 2, :], stats, t % 2,
                    L_tok[t], None, None,
                )

                @block.sync
                def _(sy, t=t):
                    W.w(sy, "y", x1_tok[t])
                    sp_tok[t] = dsp[t % 2].inc(
                        sy.dma_start(out=x1_dram[t, :, :], in_=x1t[:, t % 2, :]), 16
                    )

                @block.gpsimd
                def _(g, t=t):
                    W.w(g, "g", x1_tok[t])
                    if t >= 1:
                        W.w(g, "g", (dxb, 16 * t))
                    dxb.inc(
                        g.dma_start(out=x1b_dram[t, :, :], in_=x1t[:, t % 2, :]), 16
                    )

                if t < 6:

                    @block.gpsimd
                    def _(g, ch=t):
                        if ch >= 1:
                            W.w(g, "g", (dwC, 16 * ch + 16))
                        wch_tok[ch] = dwC.inc(
                            g.dma_start(
                                out=winT_sb[:, :, ch * 512 : (ch + 1) * 512],
                                in_=winT.ap().rearrange("(kc p) n -> p kc n", p=128)[
                                    :, :, ch * 512 : (ch + 1) * 512
                                ],
                            ),
                            16,
                        )
                        if ch == 0:
                            wch_tok[0] = dwC.inc(
                                g.dma_start(out=binp_sb[:, :], in_=binpr[:, :]), 16
                            )

            for t in range(NTO):
                b, tl = t // 8, t % 8
                slg = 2 * b + tl // 4
                sl2 = slg % 2
                tj = tl % 4

                if tj == 0:

                    @block.gpsimd
                    def _(g, b=b, tl=tl, slg=slg, sl2=sl2):
                        W.w(g, "g", (ze, 32), (dzs, 128), dc1.tok())
                        if slg >= 2:
                            W.w(g, "g", ao_tok[(slg - 2) * 4 + 3])
                        if slg == 0:
                            g.reg_load(zr, zoff_sb[:1, :1])
                        for par, z_allX in ((0, z_all1), (1, z_all2)):
                            W.w(g, "g", (cs, 4 * (par + 1)))
                            base = b * 4 * (2 * 64 * 2048) + (tl // 4) * 512
                            g.reg_mov(zb_r, base)
                            g.reg_add(zb_r, zb_r, zr)
                            zin_tok[slg] = dzi[slg % 2].inc(
                                g.dma_start(
                                    out=zin[:, sl2, par, :, :],
                                    in_=bass.AP(
                                        z_allX,
                                        zb_r,
                                        [
                                            [64 * 2048, 2],
                                            [2048, 64],
                                            [2 * 64 * 2048, 4],
                                            [1, 512],
                                        ],
                                    ),
                                ),
                                16,
                            )

                @block.sync
                def _(sy, t=t, b=b, tl=tl):
                    W.w(sy, "y", (ze, 32), (dzs, 128))
                    if t >= 2:
                        W.w(sy, "y", L_tok[t - 2])
                    x0_tok[t] = dx0[t % 2].inc(
                        sy.dma_start(
                            out=x0t[:, t % 2, :], in_=x0[b, tl * 128 : (tl + 1) * 128, :]
                        ),
                        16,
                    )

                @block.tensor
                def _(te, t=t, sl2=sl2, tj=tj, slg=slg):
                    W.w(te, "t", zin_tok[slg], wo_tok)
                    if t >= 2:
                        W.w(te, "t", L_tok[t - 2])
                    pb = t % 2
                    for kc in range(8):
                        for nh in range(2):
                            mm = te.matmul(
                                ao_ps[:, pb, nh * 512 : (nh + 1) * 512],
                                zin[:, sl2, kc % 2, kc // 2, tj * 128 : (tj + 1) * 128],
                                woT_sb[:, kc, nh * 512 : (nh + 1) * 512],
                                start=(kc == 0),
                                stop=(kc == 7),
                            )
                    ao_tok[t] = tt.inc(mm)

                @block.vector
                def _(v, t=t):
                    W.w(v, "v", ao_tok[t], x0_tok[t])
                    if t >= 2:
                        W.w(v, "v", x1_tok[t - 2], sp_tok[t - 2], dxb.tok())
                    L_tok[t] = vv.inc(
                        v.scalar_tensor_tensor(
                            Lb[:, t % 2, :],
                            x0t[:, t % 2, :],
                            2.0,
                            ao_ps[:, t % 2, 0:1024],
                            ALU.mult,
                            ALU.add,
                        )
                    )

                if t >= 1:
                    p3_tail(t - 1)

            p3_tail(NTO - 1)

            p3_v = vv.tok()
            p3_t = tt.tok()
            p3_s = ss.tok()
            p3_sp = [dsp[0].tok(), dsp[1].tok(), dxb.tok()]
            p3_x0 = [dx0[0].tok(), dx0[1].tok()]

        with ExitStack() as cd:
            ed = cd.enter_context
            lnj1 = lnj1b_t
            lnj2 = lnj2b_t
            h_tok = [None] * NTO
            hsp_tok = [None] * NTO
            hf_tok = [None] * NTO
            hfd_tok = [None] * NTO
            asn_tok = [None] * NTO
            lge_tok = [None] * NTO

            with ExitStack() as cc2:
                ec = cc2.enter_context
                woutT_sb = ec(nc.sbuf_tensor("woutT_sb", [128, 8, D], BF16))
                bout_sb = ec(nc.sbuf_tensor("bout_sb", [1, D], BF16))
                ln1g_sb = ec(nc.sbuf_tensor("ln1g_sb", [128, D], F32))
                ln1b_sb = ec(nc.sbuf_tensor("ln1b_sb", [128, D], F32))
                x1p = ec(nc.sbuf_tensor("x1p", [128, 2, 2, D], F32))
                sc4 = ec(nc.sbuf_tensor("sc4", [128, 4, 16], F32))
                at4 = ec(nc.sbuf_tensor("at4", [128, 2, 16], F32))
                mscr = ec(nc.sbuf_tensor("mscr", [128, D], F32))
                mctx = ec(nc.sbuf_tensor("mctx", [128, D], BF16))
                ctxf = ec(nc.sbuf_tensor("ctxf", [128, 8, 2, 128], BF16))
                Lb4 = ec(nc.sbuf_tensor("Lb4", [128, 2, D], F32))
                ht4 = ec(nc.sbuf_tensor("ht4", [128, 2, 2, D], F32))
                st4 = ec(nc.sbuf_tensor("st4", [128, 2, 4], F32))

                tpcb_ps = ec(nc.psum_tensor("tpcb_ps", [128, 2, 1024], BF16))
                ao2_ps = ec(nc.psum_tensor("ao2_ps", [128, 2, 512], F32))

                @block.gpsimd
                def _(g):
                    W.w(g, "g", cc_tok, p3_t, p3_s, p3_v, *p3_sp, *p3_x0)
                    W.w(g, "g", (dwC, 112))
                    dwC.inc(
                        g.dma_start(
                            out=woutT_sb[:, :, :],
                            in_=woutT.ap().rearrange("(kc p) n -> p kc n", p=128),
                        ),
                        16,
                    )
                    dwC.inc(g.dma_start(out=bout_sb[:, :], in_=boutr[:, :]), 16)

                win_tok = dwC.tok()

                @block.sync
                def _(sy):
                    W.w(sy, "y", p3_v, p3_s, p3_t)
                    dc4.inc(sy.dma_start(out=ln1g_sb[:, :], in_=ln1g[:, :]), 16)
                    dc4.inc(sy.dma_start(out=ln1b_sb[:, :], in_=ln1b[:, :]), 16)

                lng_tok = dc4.tok()

                @block.gpsimd
                def _(g):
                    W.w(g, "g", cc_tok, p3_t, p3_s, p3_v, *p3_sp, *p3_x0)
                    dwD.inc(
                        g.dma_start(
                            out=routT_sb[:, :, :],
                            in_=routT.ap().rearrange("(kc p) n -> p kc n", p=128),
                        ),
                        16,
                    )
                    dwD.inc(g.dma_start(out=rb_sb[:, :], in_=rbr[:, :]), 16)

                rt_tok = dwD.tok()

                x1p_tok = [None] * 8
                x1f_tok = [None] * 8
                qmm_last = [None] * 8
                tpcb_prev = [None]
                ctx_tok = [None] * 8
                ctxf_ev = [None] * 8
                ao2_tok = [[None, None] for _ in range(8)]
                L1_tok = [[None, None] for _ in range(8)]

                def p4_new_sync(pr):
                    tA, tB = pr, 8 + pr

                    @block.sync
                    def _(sy, pr=pr, tA=tA, tB=tB):
                        W.w(sy, "y", *p3_sp)
                        if pr == 0:
                            W.w(sy, "y", p3_v, p3_s, p3_t)
                        if pr >= 2:
                            W.w(sy, "y", L1_tok[pr - 2][1])
                        dxp[pr % 2].inc(
                            sy.dma_start(out=x1p[:, pr % 2, 0, :], in_=x1_dram[tA, :, :]),
                            16,
                        )
                        x1p_tok[pr] = dxp[pr % 2].inc(
                            sy.dma_start(out=x1p[:, pr % 2, 1, :], in_=x1_dram[tB, :, :]),
                            16,
                        )

                    @block.sync
                    def _(sy, pr=pr, tA=tA, tB=tB):
                        W.w(sy, "y", (dxb, 16 * (9 + pr)))
                        if pr == 0:
                            W.w(sy, "y", cc_tok)
                        if pr >= 2:
                            W.w(sy, "y", (tt, qmm_last[pr - 2]))
                        for tx, tg in ((0, tA), (1, tB)):
                            for kc in range(8):
                                x1f_tok[pr] = dxf[pr % 2].inc(
                                    sy.dma_start_transpose(
                                        x1fp[:, pr % 2, kc, tx, :],
                                        x1b_dram[tg, :, kc * 128 : (kc + 1) * 128],
                                    ),
                                    16,
                                )

                def p4_new_te(pr):
                    @block.tensor
                    def _(te, pr=pr):
                        W.w(te, "t", x1f_tok[pr])
                        for tx in range(2):
                            for ch in range(6):
                                ci = 6 * tx + ch
                                W.w(te, "t", wch_tok[ch], (qe, 12 * pr + ci - 2))
                                pb = ci % 3
                                for kc in range(8):
                                    te.matmul(
                                        qkvf_ps[:, pb, :],
                                        x1fp[:, pr % 2, kc, tx, :],
                                        winT_sb[:, kc, ch * 512 : (ch + 1) * 512],
                                        start=(kc == 0),
                                        stop=False,
                                    )
                                te.matmul(
                                    qkvf_ps[:, pb, :],
                                    ones1[:, :],
                                    binp_sb[:, ch * 512 : (ch + 1) * 512],
                                    start=False,
                                    stop=True,
                                ).then_inc(tt.h, 1)
                                tt.n += 1
                        qmm_last[pr] = tt.n

                    qmm_base = tt.n - 12

                    @block.scalar
                    def _(s, pr=pr, qmm_base=qmm_base):
                        if pr >= 2:
                            W.w(s, "s", ctx_tok[pr - 2])
                        for tx in range(2):
                            for ch in range(6):
                                ci = 6 * tx + ch
                                W.w(s, "s", (tt, qmm_base + ci + 1))
                                qe.inc(
                                    s.copy(
                                        qkvp[:, pr % 2, tx, ch * 512 : (ch + 1) * 512],
                                        qkvf_ps[:, ci % 3, :],
                                    )
                                )

                    qev_done = qe.tok()

                    @block.vector
                    def _(v, pr=pr, qev_done=qev_done):
                        W.w(v, "v", qev_done)
                        if pr >= 1:
                            W.w(v, "v", ctxf_ev[pr - 1])
                        qA = qkvp[:, pr % 2, 0, 0:1024].rearrange("p (g d) -> p g d", d=64)
                        kA = qkvp[:, pr % 2, 0, 1024:2048].rearrange("p (g d) -> p g d", d=64)
                        qB = qkvp[:, pr % 2, 1, 0:1024].rearrange("p (g d) -> p g d", d=64)
                        kB = qkvp[:, pr % 2, 1, 1024:2048].rearrange("p (g d) -> p g d", d=64)
                        mv = mscr[:, :].rearrange("p (g d) -> p g d", d=64)
                        for i, (qq, kk) in enumerate(
                            ((qA, kA), (qA, kB), (qB, kA), (qB, kB))
                        ):
                            W.w(v, "v", vv.tok())
                            c1 = vv.inc(v.tensor_tensor(mv, qq, kk, ALU.mult))
                            W.w(v, "v", c1)
                            vv.inc(
                                v.reduce_sum(sc4[:, i, :].unsqueeze(2), mv, axis=AX.X)
                            )
                        W.w(v, "v", vv.tok())
                        vv.inc(
                            v.tensor_tensor(
                                at4[:, 0, :], sc4[:, 0, :], sc4[:, 1, :], ALU.subtract
                            )
                        )
                        vv.inc(
                            v.tensor_tensor(
                                at4[:, 1, :], sc4[:, 2, :], sc4[:, 3, :], ALU.subtract
                            )
                        )

                    dif_tok = vv.tok()

                    @block.scalar
                    def _(s, pr=pr, dif_tok=dif_tok):
                        W.w(s, "s", dif_tok)
                        s.activation(sc4[:, 0, :], at4[:, 0, :], ACT.Sigmoid, scale=0.125)
                        s.activation(sc4[:, 1, :], at4[:, 0, :], ACT.Sigmoid, scale=-0.125)
                        s.activation(sc4[:, 2, :], at4[:, 1, :], ACT.Sigmoid, scale=0.125)
                        ss.inc(
                            s.activation(
                                sc4[:, 3, :], at4[:, 1, :], ACT.Sigmoid, scale=-0.125
                            )
                        )

                    sig_tok = ss.tok()

                    @block.vector
                    def _(v, pr=pr, sig_tok=sig_tok):
                        W.w(v, "v", sig_tok)
                        vA = qkvp[:, pr % 2, 0, 2048:3072].rearrange("p (g d) -> p g d", d=64)
                        vB = qkvp[:, pr % 2, 1, 2048:3072].rearrange("p (g d) -> p g d", d=64)
                        mcv = mctx[:, :].rearrange("p (g d) -> p g d", d=64)
                        for tx in range(2):
                            a_s = sc4[:, 2 * tx, :].unsqueeze(2).broadcast_to([128, 16, 64])
                            b_s = sc4[:, 2 * tx + 1, :].unsqueeze(2).broadcast_to([128, 16, 64])
                            cvv = qkvp[:, pr % 2, tx, 0:1024].rearrange(
                                "p (g d) -> p g d", d=64
                            )
                            W.w(v, "v", vv.tok())
                            c1 = vv.inc(v.tensor_tensor(mcv, vA, a_s, ALU.mult))
                            c2 = vv.inc(v.tensor_tensor(cvv, vB, b_s, ALU.mult))
                            W.w(v, "v", c2)
                            ctx_tok[pr] = vv.inc(v.tensor_tensor(cvv, cvv, mcv, ALU.add))

                def p4_mid(p):
                    tA, tB = p, 8 + p
                    for tx in range(2):
                        transpose8(
                            (lambda kc, p=p, tx=tx: qkvp[
                                :, p % 2, tx, kc * 128 : (kc + 1) * 128
                            ]),
                            (lambda kc, tx=tx: ctxf[:, kc, tx, :]),
                            tpcb_ps, idn, [ctx_tok[p]], tpcb_prev,
                        )
                    ctxf_ev[p] = ss.tok()

                    for tx, tg in ((0, tA), (1, tB)):

                        @block.tensor
                        def _(te, p=p, tx=tx):
                            W.w(te, "t", ctxf_ev[p], win_tok)
                            if tx == 1:
                                W.w(te, "t", L1_tok[p][0])
                            elif p >= 1:
                                W.w(te, "t", L1_tok[p - 1][1])
                            for nh in range(2):
                                for kc in range(8):
                                    te.matmul(
                                        ao2_ps[:, nh, :],
                                        ctxf[:, kc, tx, :],
                                        woutT_sb[:, kc, nh * 512 : (nh + 1) * 512],
                                        start=(kc == 0),
                                        stop=False,
                                    )
                                mm = te.matmul(
                                    ao2_ps[:, nh, :],
                                    ones1[:, :],
                                    bout_sb[:, nh * 512 : (nh + 1) * 512],
                                    start=False,
                                    stop=True,
                                )
                            ao2_tok[p][tx] = tt.inc(mm)

                        @block.vector
                        def _(v, p=p, tx=tx, tg=tg):
                            W.w(v, "v", ao2_tok[p][tx], x1p_tok[p])
                            if p >= 1:
                                W.w(v, "v", h_tok[tg - 1])
                            if p >= 2:
                                W.w(v, "v", dhb.tok(), dhs[0].tok(), dhs[1].tok())
                            L1_tok[p][tx] = vv.inc(
                                v.scalar_tensor_tensor(
                                    Lb4[:, tx, :],
                                    x1p[:, p % 2, tx, :],
                                    1.0,
                                    ao2_ps[:, 0:2, :],
                                    ALU.mult,
                                    ALU.add,
                                )
                            )

                        h_tok[tg] = emit_ln(
                            Lb4[:, tx, :], ht4[:, p % 2, tx, :], st4, tx,
                            L1_tok[p][tx], (ln1g_sb, ln1b_sb), [lng_tok],
                        )

                def p4_old(p):
                    for tx, tg in ((0, p), (1, 8 + p)):

                        @block.sync
                        def _(sy, p=p, tx=tx, tg=tg):
                            W.w(sy, "y", h_tok[tg])
                            hsp_tok[tg] = dhs[(2 * p + tx) % 2].inc(
                                sy.dma_start(
                                    out=h_dram[tg, :, :], in_=ht4[:, p % 2, tx, :]
                                ),
                                16,
                            )

                        @block.gpsimd
                        def _(g, p=p, tx=tx, tg=tg):
                            W.w(g, "g", h_tok[tg])
                            if p >= 1 and tx == 0:
                                W.w(g, "g", (dhb, 32 * p))
                            hf_tok[tg] = dhb.inc(
                                g.dma_start(
                                    out=hb_dram[tg, :, :], in_=ht4[:, p % 2, tx, :]
                                ),
                                16,
                            )

                    for tx, tg in ((0, p), (1, 8 + p)):

                        @block.sync
                        def _(sy, p=p, tx=tx, tg=tg):
                            W.w(sy, "y", hf_tok[8 + p])
                            if p >= 1:
                                W.w(sy, "y", (dhf[tx], 128 * p))
                            if p == 0 and tx == 0:
                                W.w(
                                    sy, "y", cc_tok, p3_t, p3_s, p3_v,
                                    *p3_sp, *p3_x0,
                                )
                            for kc in range(8):
                                hfd_tok[tg] = dhf[tx].inc(
                                    sy.dma_start_transpose(
                                        hf[:, kc, tg * 128 : (tg + 1) * 128],
                                        hb_dram[tg, :, kc * 128 : (kc + 1) * 128],
                                    ),
                                    16,
                                )

                def p4_rt(p):
                    for tx, tg in ((0, p), (1, 8 + p)):

                        @block.tensor
                        def _(te, p=p, tx=tx, tg=tg):
                            W.w(te, "t", hfd_tok[tg], rt_tok)
                            prev = 8 + p - 1 if tx == 0 else p
                            if p >= 1 or tx == 1:
                                W.w(te, "t", lge_tok[prev])
                            for kc in range(8):
                                te.matmul(
                                    rt_ps[:, 0:8],
                                    hf[:, kc, tg * 128 : (tg + 1) * 128],
                                    routT_sb[:, kc, :],
                                    start=(kc == 0),
                                    stop=False,
                                )
                            tt.inc(
                                te.matmul(
                                    rt_ps[:, 0:8], ones1[:, :], rb_sb[:, :],
                                    start=False, stop=True,
                                )
                            )

                        lg_tok = tt.tok()

                        @block.scalar
                        def _(s, p=p, tx=tx, tg=tg, lg_tok=lg_tok):
                            W.w(s, "s", lg_tok)
                            if p >= 1:
                                W.w(s, "s", asn_tok[tg - 1])
                            lge_tok[tg] = ss.inc(
                                s.copy(lgsb[:, tx, :], rt_ps[:, 0:8])
                            )

                        @block.vector
                        def _(v, tx=tx, tg=tg):
                            W.w(v, "v", lge_tok[tg], dc1.tok())
                            c1 = vv.inc(
                                v.reduce_max(rst[:, 0:1], lgsb[:, tx, :], axis=AX.X)
                            )
                            W.w(v, "v", c1)
                            c2 = vv.inc(
                                v.tensor_scalar(
                                    r8[:, 0, :], lgsb[:, tx, :], rst[:, 0:1], None,
                                    ALU.is_equal,
                                )
                            )
                            W.w(v, "v", c2)
                            c3 = vv.inc(
                                v.tensor_tensor(
                                    r8[:, 1, :], r8[:, 0, :], iot[:, :], ALU.mult
                                )
                            )
                            W.w(v, "v", c3)
                            c4 = vv.inc(
                                v.tensor_reduce(rst[:, 1:2], r8[:, 1, :], AX.X, ALU.min)
                            )
                            W.w(v, "v", c4)
                            c5 = vv.inc(
                                v.tensor_scalar(
                                    r8[:, 0, :], iot[:, :], rst[:, 1:2], None,
                                    ALU.is_equal,
                                )
                            )
                            W.w(v, "v", c5)
                            c6 = vv.inc(
                                v.scalar_tensor_tensor(
                                    r8[:, 2, :], r8[:, 0, :], -1e30, lgsb[:, tx, :],
                                    ALU.mult, ALU.add,
                                )
                            )
                            W.w(v, "v", c6)
                            c7 = vv.inc(v.reduce_max(rst[:, 2:3], r8[:, 2, :], axis=AX.X))
                            W.w(v, "v", c7)
                            c8 = vv.inc(
                                v.tensor_scalar(
                                    r8[:, 0, :], r8[:, 2, :], rst[:, 2:3], None,
                                    ALU.is_equal,
                                )
                            )
                            W.w(v, "v", c8)
                            c9 = vv.inc(
                                v.tensor_tensor(
                                    r8[:, 1, :], r8[:, 0, :], iot[:, :], ALU.mult
                                )
                            )
                            W.w(v, "v", c9)
                            c10 = vv.inc(
                                v.tensor_reduce(rst[:, 3:4], r8[:, 1, :], AX.X, ALU.min)
                            )
                            W.w(v, "v", c10)
                            c11 = vv.inc(
                                v.tensor_tensor(
                                    rst[:, 4:5], rst[:, 1:2], rst[:, 3:4], ALU.max
                                )
                            )
                            W.w(v, "v", c11)
                            asn_tok[tg] = vv.inc(
                                v.tensor_scalar(
                                    assign[:, tg : tg + 1], rst[:, 4:5], 1000.0, None,
                                    ALU.add,
                                )
                            )

                for pr in range(11):
                    if pr < 8:
                        p4_new_sync(pr)
                    if 2 <= pr <= 9:
                        p4_old(pr - 2)
                    if 1 <= pr <= 8:
                        p4_mid(pr - 1)
                    if pr < 8:
                        p4_new_te(pr)
                    if 3 <= pr <= 10:
                        p4_rt(pr - 3)

                p4_t = tt.tok()
                p4_s = ss.tok()
                p4_v = vv.tok()
                rtr_s = ss.tok()
                rtr_v = vv.tok()
                p4_hs = [dhs[0].tok(), dhs[1].tok(), dhb.tok()]
                p4_xp = [dxp[0].tok(), dxp[1].tok(), dxf[0].tok(), dxf[1].tok()]

            cbE.close()
            cdb = cd.enter_context(ExitStack())
            f_acc = cdb.enter_context(nc.sbuf_tensor("f_acc", [128, NTO, D], BF16))
            cd2 = cdb.enter_context(ExitStack())
            if True:
                ed2 = cd2.enter_context
                hef = ed2(nc.sbuf_tensor("hef", [128, 8, 2 * POWN], BF16))
                w1sb = ed2(nc.sbuf_tensor("w1sb", [128, 2, 8, D], BF16))
                w2sb = ed2(nc.sbuf_tensor("w2sb", [128, 2, 8, D], BF16))
                b1e = ed2(nc.sbuf_tensor("b1e", [128, 2, 8], F32))
                b2e = ed2(nc.sbuf_tensor("b2e", [1, 2, D], BF16))
                mcol = ed2(nc.sbuf_tensor("mcol", [128, 1], F32))
                he_ps = ed2(nc.psum_tensor("he_ps", [128, 2, 512], F32))
                oe_ps = ed2(nc.psum_tensor("oe_ps", [128, 2, 1024], F32))

                ln2g_sb = ed2(nc.sbuf_tensor("ln2g_sb", [128, D], BF16))
                ln2b_sb = ed2(nc.sbuf_tensor("ln2b_sb", [128, D], BF16))
                ht6 = ed2(nc.sbuf_tensor("ht6", [128, 2, D], F32))
                x1t6 = ed2(nc.sbuf_tensor("x1t6", [128, 2, D], F32))
                f2b = ed2(nc.sbuf_tensor("f2b", [128, 2, D], F32))
                f2o = ed2(nc.sbuf_tensor("f2o", [128, 2, D], BF16))
                st6 = ed2(nc.sbuf_tensor("st6", [128, 2, 4], F32))
                L3b = f2b
                obuf = ht6

                @block.gpsimd
                def _(g):
                    W.w(
                        g, "g", p4_t, p4_s, p4_v, *p4_hs, *p4_xp,
                        win_tok, lng_tok,
                    )
                    dc6.inc(g.dma_start(out=ln2g_sb[:, :], in_=ln2g[:, :]), 16)
                    dc6.inc(g.dma_start(out=ln2b_sb[:, :], in_=ln2b[:, :]), 16)

                ln2_tok = dc6.tok()

                h6_tok = [None] * NTO
                L2_tok = [None] * NTO
                f2_tok = [None] * NTO
                L3_tok = [None] * NTO
                o_tok = [None] * NTO
                od_tok = [None] * NTO

                def p6_tile(t):
                    @block.sync
                    def _(sy, t=t):
                        W.w(sy, "y", *p4_hs, *p3_sp)
                        if t >= 2:
                            W.w(sy, "y", L2_tok[t - 2], L3_tok[t - 2], od_tok[t - 2])
                        dp6[t % 2].inc(
                            sy.dma_start(out=ht6[:, t % 2, :], in_=h_dram[t, :, :]), 16
                        )
                        h6_tok[t] = dp6[t % 2].inc(
                            sy.dma_start(out=x1t6[:, t % 2, :], in_=x1_dram[t, :, :]),
                            16,
                        )

                    @block.vector
                    def _(v, t=t):
                        W.w(v, "v", h6_tok[t], (ff, 96 + t + 1), ln2_tok)
                        if t >= 2:
                            W.w(v, "v", o_tok[t - 2])
                        L2_tok[t] = vv.inc(
                            v.tensor_tensor(
                                f2b[:, t % 2, :], ht6[:, t % 2, :], f_acc[:, t, :],
                                ALU.add,
                            )
                        )

                    f2_tok[t] = emit_ln(
                        f2b[:, t % 2, :], f2o[:, t % 2, :], st6, t % 2,
                        L2_tok[t], (ln2g_sb, ln2b_sb), None,
                    )

                    @block.vector
                    def _(v, t=t):
                        W.w(v, "v", f2_tok[t])
                        if t >= 2:
                            W.w(v, "v", od_tok[t - 2])
                        L3_tok[t] = vv.inc(
                            v.tensor_tensor(
                                L3b[:, t % 2, :], x1t6[:, t % 2, :], f2o[:, t % 2, :],
                                ALU.add,
                            )
                        )

                    o_tok[t] = emit_ln(
                        L3b[:, t % 2, :], obuf[:, t % 2, :], st6, t % 2,
                        L3_tok[t], None, None,
                    )

                    @block.sync
                    def _(sy, t=t):
                        W.w(sy, "y", o_tok[t])
                        b, tl = t // 8, t % 8
                        od_tok[t] = dou[t % 2].inc(
                            sy.dma_start(
                                out=out[b, tl * 128 : (tl + 1) * 128, :],
                                in_=obuf[:, t % 2, :],
                            ),
                            16,
                        )

                wg_tok = [None] * 7
                wy_tok = [None] * 7
                oe_tok = [[None] * NTO for _ in range(7)]

                for ei in range(7):
                    e_ = ei + 1
                    pe = ei % 2

                    @block.gpsimd
                    def _(g, ei=ei, e_=e_, pe=pe):
                        if ei == 0:
                            W.w(
                                g, "g", p4_t, p4_s, p4_v, *p4_hs, *p4_xp,
                                win_tok, lng_tok,
                            )
                        if ei >= 2:
                            W.w(g, "g", oe_tok[ei - 2][NTO - 1])
                        dwe[pe].inc(
                            g.dma_start(
                                out=w1sb[:, pe, :, :],
                                in_=w1T[e_, :, :].rearrange("(kc p) n -> p kc n", p=128),
                            ),
                            16,
                        )
                        dwe[pe].inc(
                            g.dma_start(
                                out=w2sb[:, pe, :, :],
                                in_=w2T[e_, :, :].rearrange("(kc p) n -> p kc n", p=128),
                            ),
                            16,
                        )
                        wg_tok[ei] = dwe[pe].inc(
                            g.dma_start(out=b2e[:, pe, :], in_=b2r[e_ : e_ + 1, :]), 16
                        )

                    @block.sync
                    def _(sy, ei=ei, e_=e_, pe=pe):
                        if ei == 0:
                            W.w(
                                sy, "y", p4_t, p4_s, p4_v, *p4_hs, *p4_xp,
                                win_tok, lng_tok,
                            )
                        if ei >= 2:
                            W.w(sy, "y", (gg, 32 * (ei - 1)))
                        wy_tok[ei] = dbe[pe].inc(
                            sy.dma_start(out=b1e[:, pe, :], in_=b1c[e_, :, :]), 16
                        )

                    @block.tensor
                    def _(te, ei=ei, pe=pe):
                        W.w(te, "t", wg_tok[ei], wy_tok[ei])
                        if ei == 0:
                            W.w(te, "t", rtr_s, rtr_v)
                        for m in range(8):
                            for t4 in range(4):
                                ci = 4 * m + t4
                                W.w(te, "t", (gg, 32 * ei + ci - 1))
                                for kc in range(8):
                                    mm = te.matmul(
                                        he_ps[:, ci % 2, :],
                                        w1sb[:, pe, kc, m * 128 : (m + 1) * 128],
                                        hf[:, kc, t4 * 512 : (t4 + 1) * 512],
                                        start=(kc == 0),
                                        stop=(kc == 7),
                                    )
                                tt.inc(mm)

                    hmm_base = tt.n - 32

                    @block.scalar
                    def _(s, ei=ei, pe=pe, hmm_base=hmm_base):
                        for m in range(8):
                            for t4 in range(4):
                                ci = 4 * m + t4
                                W.w(s, "s", (tt, hmm_base + ci + 1))
                                if ei >= 1:
                                    W.w(s, "s", oe_tok[ei - 1][NTO - 1])
                                gg.inc(
                                    s.activation(
                                        hef[:, m, t4 * 512 : (t4 + 1) * 512],
                                        he_ps[:, ci % 2, :],
                                        ACT.Gelu,
                                        bias=b1e[:, pe, m : m + 1],
                                    )
                                )

                    @block.tensor
                    def _(te, ei=ei, pe=pe):
                        W.w(te, "t", (gg, 32 * (ei + 1)))
                        for t in range(NTO):
                            W.w(te, "t", (ff, 16 * ei + t - 1))
                            for nh in range(2):
                                for kc in range(8):
                                    te.matmul(
                                        oe_ps[:, t % 2, nh * 512 : (nh + 1) * 512],
                                        hef[:, kc, t * 128 : (t + 1) * 128],
                                        w2sb[:, pe, kc, nh * 512 : (nh + 1) * 512],
                                        start=(kc == 0),
                                        stop=False,
                                    )
                                mm = te.matmul(
                                    oe_ps[:, t % 2, nh * 512 : (nh + 1) * 512],
                                    ones1[:, :],
                                    b2e[:, pe, nh * 512 : (nh + 1) * 512],
                                    start=False,
                                    stop=True,
                                )
                            oe_tok[ei][t] = tt.inc(mm)

                    for t in range(NTO):

                        @block.vector
                        def _(v, ei=ei, e_=e_, t=t):
                            W.w(v, "v", oe_tok[ei][t], asn_tok[t], ff.tok())
                            mc = vv.inc(
                                v.tensor_single_scalar(
                                    mcol[:, 0:1], assign[:, t : t + 1], float(e_),
                                    ALU.is_equal,
                                )
                            )
                            W.w(v, "v", mc)
                            if ei == 0:
                                ff.inc(
                                    v.tensor_scalar(
                                        f_acc[:, t, :], oe_ps[:, t % 2, 0:1024],
                                        mcol[:, 0:1], None, ALU.mult,
                                    )
                                )
                            else:
                                ff.inc(
                                    v.scalar_tensor_tensor(
                                        f_acc[:, t, :],
                                        oe_ps[:, t % 2, 0:1024],
                                        mcol[:, 0:1],
                                        f_acc[:, t, :],
                                        ALU.mult,
                                        ALU.add,
                                    )
                                )

                        if ei == 6:
                            p6_tile(t)


                @block.gpsimd
                def _(g):
                    for t in range(NTO):
                        W.w(g, "g", od_tok[t])

    return nc


def _prep_inputs(inputs):
    x = np.ascontiguousarray(inputs["L0_attn_x"], dtype=np.float32)
    Wq = inputs["L0_attn_Wq"]
    bq = inputs["L0_attn_bq"]
    Wk = inputs["L0_attn_Wk"]
    bk = inputs["L0_attn_bk"]
    Wv = inputs["L0_attn_Wv"]
    bv = inputs["L0_attn_bv"]
    Wo = inputs["L0_attn_Wo"]
    norm_w = inputs["L0_attn_norm_w"]

    P64 = np.concatenate([np.arange(0, 64, 2), np.arange(1, 64, 2)])
    permfull = np.concatenate([h * 64 + P64 for h in range(16)])
    Wqp = Wq[permfull, :]
    Wkp = Wk[permfull, :]
    bqp = bq[permfull]
    bkp = bk[permfull]
    nwp = norm_w[P64]

    inv = 1.0 / (10000.0 ** (np.arange(0, 64, 2, dtype=np.float64) / 64.0))
    tpos = np.arange(S, dtype=np.float64)
    freqs = np.outer(tpos, inv)
    emb = np.concatenate([freqs, freqs], axis=-1)
    cos = np.cos(emb).astype(np.float32)
    sin = np.sin(emb).astype(np.float32)
    cE, sE = cos[:, 0::2], sin[:, 0::2]
    cO, sO = cos[:, 1::2], sin[:, 1::2]
    ropek = np.ascontiguousarray(
        np.concatenate([cE, sE, cO, sO], axis=1), dtype=np.float32
    )
    ropeq = (ropek / 64.0).astype(np.float32)

    masksv = np.zeros((128, 4, 512), np.float32)
    ar = np.arange(512)
    for m in range(4):
        for p in range(128):
            masksv[p, m, :] = (ar >= p + 128 * m).astype(np.float32)

    common = {
        "ropeq": ropeq,
        "ropek": ropek,
        "normw": np.ascontiguousarray(
            np.broadcast_to(np.tile(nwp, 8)[None, :], (128, 512)), dtype=np.float32
        ),
        "woT": np.ascontiguousarray(Wo.T, dtype=np.float32),
        "masksi": masksv,
        "identi": np.eye(128, dtype=np.float32),
        "winT": np.ascontiguousarray(inputs["L0_ffn_in_proj_w"].T, dtype=np.float32),
        "binpr": np.ascontiguousarray(
            inputs["L0_ffn_in_proj_b"][None, :], dtype=np.float32
        ),
        "woutT": np.ascontiguousarray(inputs["L0_ffn_out_w"].T, dtype=np.float32),
        "boutr": np.ascontiguousarray(inputs["L0_ffn_out_b"][None, :], dtype=np.float32),
        "ln1g": np.ascontiguousarray(
            np.broadcast_to(inputs["L0_ffn_ln1_g"][None, :], (128, D)), dtype=np.float32
        ),
        "ln1b": np.ascontiguousarray(
            np.broadcast_to(inputs["L0_ffn_ln1_b"][None, :], (128, D)), dtype=np.float32
        ),
        "ln2g": np.ascontiguousarray(
            np.broadcast_to(inputs["L0_ffn_ln2_g"][None, :], (128, D)), dtype=np.float32
        ),
        "ln2b": np.ascontiguousarray(
            np.broadcast_to(inputs["L0_ffn_ln2_b"][None, :], (128, D)), dtype=np.float32
        ),
        "routT": np.ascontiguousarray(inputs["L0_ffn_router_w"].T, dtype=np.float32),
        "rbr": np.ascontiguousarray(
            inputs["L0_ffn_router_b"][None, :], dtype=np.float32
        ),
        "iotam": np.ascontiguousarray(
            np.broadcast_to(
                (np.arange(8, dtype=np.float32) - 1000.0)[None, :], (128, 8)
            ),
            dtype=np.float32,
        ),
        "w1T": np.ascontiguousarray(
            inputs["L0_ffn_w1"].transpose(0, 2, 1), dtype=np.float32
        ),
        "b1c": np.ascontiguousarray(
            inputs["L0_ffn_b1"].reshape(8, 8, 128).transpose(0, 2, 1), dtype=np.float32
        ),
        "w2T": np.ascontiguousarray(
            inputs["L0_ffn_w2"].transpose(0, 2, 1), dtype=np.float32
        ),
        "b2r": np.ascontiguousarray(inputs["L0_ffn_b2"], dtype=np.float32),
    }

    in_maps = []
    for c in range(NCORE):
        bc = c // 4
        h0 = 4 * (c % 4) * 64
        sel = slice(h0, h0 + 256)
        m = dict(common)
        m["xfm"] = np.ascontiguousarray(x[bc].T, dtype=np.float32)
        m["x0"] = np.ascontiguousarray(
            x[:, c * POWN : (c + 1) * POWN, :], dtype=np.float32
        )
        m["wqkvT"] = np.ascontiguousarray(
            np.concatenate([Wqp[sel].T, Wkp[sel].T, Wv[sel].T], axis=1),
            dtype=np.float32,
        )
        m["bqkvr"] = np.ascontiguousarray(
            np.concatenate([bqp[sel], bkp[sel], bv[sel]])[None, :], dtype=np.float32
        )
        m["zoffi"] = np.array(
        [[(c // 2) * (NCORE * 2 * 64 * 2048) + (c % 2) * 1024]], dtype=np.uint32
    )
        in_maps.append(m)
    return in_maps


def kernel(**inputs):
    in_maps = _prep_inputs(inputs)
    if "nc" not in _CACHE:
        _CACHE["nc"] = build_nc()
    res = run_bass_kernel_spmd(_CACHE["nc"], in_maps, core_ids=list(range(NCORE)))
    return np.concatenate([r["out"] for r in res.results], axis=1)


# version 16
# speedup vs baseline: 1.0146x; 1.0146x over previous
from contextlib import ExitStack

import numpy as np

import concourse.bass as bass
import concourse.mybir as mybir
from concourse.bass_utils import run_bass_kernel_spmd

F32 = mybir.dt.float32
BF16 = mybir.dt.bfloat16
U32 = mybir.dt.uint32
ALU = mybir.AluOpType
ACT = mybir.ActivationFunctionType
AX = mybir.AxisListType

B = 2
S = 8192
D = 1024
NCORE = 8
POWN = S // NCORE
NT1 = S // 128
NTO = 2 * POWN // 128
NQ = S // 512
NKT = S // 128
GSZ = 3
EPS_RMS = float(np.finfo(np.float32).eps)
EPS_LN = 1e-5

_CACHE = {}


class Sem:
    def __init__(self, nc, ctx, name):
        self.h = ctx.enter_context(nc.semaphore(name))
        self.n = 0
        self.name = name

    def inc(self, instr, k=1):
        instr.then_inc(self.h, k)
        self.n += k
        return (self, self.n)

    def tok(self):
        return (self, self.n)


class Waits:
    def __init__(self):
        self.seen = {}

    def w(self, eng, ename, *deps):
        for d in deps:
            if d is None:
                continue
            sem, val = d
            if val <= 0:
                continue
            key = (ename, sem.name)
            if self.seen.get(key, 0) < val:
                eng.wait_ge(sem.h, val)
                self.seen[key] = val


def p2_groups():
    groups = []
    for u in range(4):
        for qi in range(NQ):
            nkt = 4 * (qi + 1)
            kts = list(range(nkt))
            for i in range(0, nkt, GSZ):
                groups.append((u, qi, kts[i : i + GSZ]))
    eng_of = []
    load = {0: 0.0, 1: 0.0}
    for u, qi, kl in groups:
        nkt = 4 * (qi + 1)
        diag = any(kt >= nkt - 4 for kt in kl)
        w = len(kl) * 512
        cs_ = (172 + w) / 1.2
        cv_ = (120 + w / 2 + 151 + w) / 0.96 + ((58 + w / 2) / 0.96 if diag else 0)
        en = 1 if diag else (0 if load[0] + cs_ <= load[1] + cv_ else 1)
        eng_of.append(en)
        load[en] += cs_ if en == 0 else cv_
    return groups, eng_of


def build_nc():
    nc = bass.Bass("TRN2", target_bir_lowering=False)

    def inp(name, shape, dt=F32):
        return nc.dram_tensor(name, shape, dt, kind="ExternalInput")

    xfm = inp("xfm", [D, S])
    x0 = inp("x0", [2, POWN, D])
    wqkvT = inp("wqkvT", [D, 768])
    bqkvr = inp("bqkvr", [1, 768])
    ropeq = inp("ropeq", [S, 128])
    ropek = inp("ropek", [S, 128])
    normw = inp("normw", [128, 512])
    woT = inp("woT", [D, D])
    masksi = inp("masksi", [128, 4, 512])
    identi = inp("identi", [128, 128])
    winT = inp("winT", [D, 3072])
    binpr = inp("binpr", [1, 3072])
    woutT = inp("woutT", [D, D])
    boutr = inp("boutr", [1, D])
    ln1g = inp("ln1g", [128, D])
    ln1b = inp("ln1b", [128, D])
    ln2g = inp("ln2g", [128, D])
    ln2b = inp("ln2b", [128, D])
    routT = inp("routT", [D, 8])
    rbr = inp("rbr", [1, 8])
    iotam = inp("iotam", [128, 8])
    w1T = inp("w1T", [8, D, D])
    b1c = inp("b1c", [8, 128, 8])
    w2T = inp("w2T", [8, D, D])
    b2r = inp("b2r", [8, D])
    zoffi = inp("zoffi", [1, 1], U32)

    out = nc.dram_tensor("out", [2, POWN, D], F32, kind="ExternalOutput")

    z_send1 = nc.dram_tensor("z_send1", [4, 2, 64, 2048], BF16)
    z_send2 = nc.dram_tensor("z_send2", [4, 2, 64, 2048], BF16)
    z_all1 = nc.dram_tensor("z_all1", [4, NCORE, 2, 64, 2048], BF16)
    z_all2 = nc.dram_tensor("z_all2", [4, NCORE, 2, 64, 2048], BF16)
    x1_dram = nc.dram_tensor("x1_dram", [NTO, 128, D], F32)
    h_dram = nc.dram_tensor("h_dram", [NTO, 128, D], F32)
    x1b_dram = nc.dram_tensor("x1b_dram", [NTO, 128, D], BF16)
    hb_dram = nc.dram_tensor("hb_dram", [NTO, 128, D], BF16)

    groups, eng_of = p2_groups()
    NG = len(groups)
    sq_done = [None] * NG
    pat_done = [None] * NG
    cnt = {0: 0, 1: 0}
    for gi, (u, qi, kl) in enumerate(groups):
        en = eng_of[gi]
        nkt = 4 * (qi + 1)
        diag = any(kt >= nkt - 4 for kt in kl)
        cnt[en] += 1
        sq_done[gi] = (en, cnt[en])
        if en == 1:
            cnt[en] += 1
        if diag:
            cnt[en] += 1
        pat_done[gi] = (en, cnt[en])

    W = Waits()

    with ExitStack() as ctx:
        e = ctx.enter_context

        tt = Sem(nc, ctx, "tt")
        ts = Sem(nc, ctx, "ts")
        tz = Sem(nc, ctx, "tz")
        ss = Sem(nc, ctx, "ss")
        vv = Sem(nc, ctx, "vv")
        ze = Sem(nc, ctx, "ze")
        qe = Sem(nc, ctx, "qe")
        gg = Sem(nc, ctx, "gg")
        ff = Sem(nc, ctx, "ff")
        cs = Sem(nc, ctx, "cs")
        dwA = Sem(nc, ctx, "dwA")
        dwB = Sem(nc, ctx, "dwB")
        dwC = Sem(nc, ctx, "dwC")
        dwD = Sem(nc, ctx, "dwD")
        dc1 = Sem(nc, ctx, "dc1")
        dc4 = Sem(nc, ctx, "dc4")
        dc6 = Sem(nc, ctx, "dc6")
        dzs = Sem(nc, ctx, "dzs")
        dxs = [Sem(nc, ctx, f"dxs{i}") for i in range(2)]
        dro = [Sem(nc, ctx, f"dro{i}") for i in range(2)]
        dzi = [Sem(nc, ctx, f"dzi{i}") for i in range(2)]
        dx0 = [Sem(nc, ctx, f"dx0{i}") for i in range(2)]
        dsp = [Sem(nc, ctx, f"dsp{i}") for i in range(2)]
        dxp = [Sem(nc, ctx, f"dxp{i}") for i in range(2)]
        dhs = [Sem(nc, ctx, f"dhs{i}") for i in range(2)]
        dwe = [Sem(nc, ctx, f"dwe{i}") for i in range(2)]
        dbe = [Sem(nc, ctx, f"dbe{i}") for i in range(2)]
        dp6 = [Sem(nc, ctx, f"dp6{i}") for i in range(2)]
        dxb = Sem(nc, ctx, "dxb")
        dhb = Sem(nc, ctx, "dhb")
        dxf = [Sem(nc, ctx, f"dxf{i}") for i in range(2)]
        dhf = [Sem(nc, ctx, f"dhf{i}") for i in range(2)]
        dou = [Sem(nc, ctx, f"dou{i}") for i in range(2)]

        ones1 = e(nc.sbuf_tensor("ones1", [1, 128], BF16))
        epsr = e(nc.sbuf_tensor("epsr", [128, 1], F32))
        epsl = e(nc.sbuf_tensor("epsl", [128, 1], F32))
        idn = e(nc.sbuf_tensor("idn", [128, 128], BF16))
        idn32 = e(nc.sbuf_tensor("idn32", [128, 128], F32))
        zoff_sb = e(nc.sbuf_tensor("zoff_sb", [1, 1], U32))
        iot = e(nc.sbuf_tensor("iot", [128, 8], F32))
        block = e(nc.Block())

        @block.gpsimd
        def _(g):
            g.memset(ones1[:, :], 1.0)
            g.memset(epsr[:, :], EPS_RMS)
            g.memset(epsl[:, :], EPS_LN)
            dwA.inc(g.dma_start(out=idn[:, :], in_=identi[:, :]), 16)

        @block.sync
        def _(sy):
            dc1.inc(sy.dma_start(out=idn32[:, :], in_=identi[:, :]), 16)
            dc1.inc(sy.dma_start(out=zoff_sb[:, :], in_=zoffi[:, :]), 16)
            dc1.inc(sy.dma_start(out=iot[:, :], in_=iotam[:, :]), 16)


        def transpose8(src_fn, dst_fn, tp, idmat, dep_toks, prev_evac):
            mm_toks = [None, None]

            @block.tensor
            def _(te):
                W.w(te, "t", *dep_toks)
                if prev_evac[0] is not None:
                    W.w(te, "t", prev_evac[0])
                for kc in range(8):
                    tr = te.transpose(
                        tp[:, kc // 4, (kc % 4) * 128 : (kc % 4 + 1) * 128],
                        src_fn(kc),
                        idmat[:, :],
                    )
                    if kc % 4 == 3:
                        mm_toks[kc // 4] = tt.inc(tr)

            @block.scalar
            def _(s):
                for half in range(2):
                    W.w(s, "s", mm_toks[half])
                    for kc in range(4 * half, 4 * half + 4):
                        cp = s.copy(
                            dst_fn(kc),
                            tp[:, kc // 4, (kc % 4) * 128 : (kc % 4 + 1) * 128],
                        )
                prev_evac[0] = ss.inc(cp)

            return prev_evac[0]

        def emit_ln(Lsrc, dest, stats, pb, dep_tok, gains, extra_v):

            @block.vector
            def _(v):
                W.w(v, "v", dep_tok)
                c0 = vv.inc(v.tensor_tensor(lnj1[:, :], Lsrc, Lsrc, ALU.mult))
                W.w(v, "v", c0)
                vv.inc(v.reduce_sum(stats[:, pb, 1:2], lnj1[:, :], axis=AX.X))
                vv.inc(v.reduce_sum(stats[:, pb, 0:1], Lsrc, axis=AX.X))
                W.w(v, "v", vv.tok())
                m1 = vv.inc(
                    v.tensor_scalar(
                        stats[:, pb, 2:4], stats[:, pb, 0:2], 1.0 / D, None, ALU.mult
                    )
                )
                W.w(v, "v", m1)
                m2 = vv.inc(
                    v.tensor_tensor(
                        stats[:, pb, 0:1], stats[:, pb, 2:3], stats[:, pb, 2:3], ALU.mult
                    )
                )
                W.w(v, "v", m2)
                vv.inc(
                    v.scalar_tensor_tensor(
                        stats[:, pb, 1:2], stats[:, pb, 3:4], EPS_LN,
                        stats[:, pb, 0:1], ALU.add, ALU.subtract,
                    )
                )

            var_tok = vv.tok()

            @block.scalar
            def _(s):
                W.w(s, "s", var_tok)
                ss.inc(
                    s.activation(stats[:, pb, 1:2], stats[:, pb, 1:2], ACT.Sqrt)
                )

            sd_tok = ss.tok()

            fin = [None]

            @block.vector
            def _(v):
                W.w(v, "v", sd_tok)
                if extra_v is not None:
                    W.w(v, "v", *extra_v)
                i1 = vv.inc(v.reciprocal(stats[:, pb, 1:2], stats[:, pb, 1:2]))
                W.w(v, "v", i1)
                i2 = vv.inc(
                    v.tensor_tensor(
                        stats[:, pb, 0:1], stats[:, pb, 2:3], stats[:, pb, 1:2], ALU.mult
                    )
                )
                W.w(v, "v", i2)
                ap = vv.inc(
                    v.tensor_scalar(
                        dest, Lsrc, stats[:, pb, 1:2], stats[:, pb, 0:1],
                        ALU.mult, ALU.subtract,
                    )
                )
                if gains is not None:
                    gsb, bsb = gains
                    W.w(v, "v", ap)
                    g1 = vv.inc(v.tensor_tensor(dest, dest, gsb[:, :], ALU.mult))
                    W.w(v, "v", g1)
                    ap = vv.inc(v.tensor_tensor(dest, dest, bsb[:, :], ALU.add))
                fin[0] = ap

            return fin[0]

        with ExitStack() as ca:
            ea = ca.enter_context
            qf = ea(nc.sbuf_tensor("qf", [128, 2, S], BF16))
            kf = ea(nc.sbuf_tensor("kf", [128, 2, S], BF16))
            vt = ea(nc.sbuf_tensor("vt", [128, 4, NKT, 64], BF16))
            zsb = ea(nc.sbuf_tensor("zsb", [128, 2, S], BF16))
            xs = ea(nc.sbuf_tensor("xs", [128, 2, 8, 512], BF16))
            wqkv_sb = ea(nc.sbuf_tensor("wqkv_sb", [128, 8, 768], BF16))
            bqkv_sb = ea(nc.sbuf_tensor("bqkv_sb", [1, 768], BF16))
            ropq_sb = ea(nc.sbuf_tensor("ropq_sb", [128, 2, 4, 128], F32))
            ropk_sb = ea(nc.sbuf_tensor("ropk_sb", [128, 2, 4, 128], F32))
            normw_sb = ea(nc.sbuf_tensor("normw_sb", [128, 512], F32))
            qkvt = ea(nc.sbuf_tensor("qkvt", [128, 2, 768], F32))
            rqk = ea(nc.sbuf_tensor("rqk", [128, 4, 512], BF16))
            sqb = ea(nc.sbuf_tensor("sqb", [128, 512], F32))
            rscr = ea(nc.sbuf_tensor("rscr", [128, 512], F32))
            ssum = ea(nc.sbuf_tensor("ssum", [128, 2, 8], F32))
            rsb = ea(nc.sbuf_tensor("rsb", [128, 2, 8], F32))
            qn = ea(nc.sbuf_tensor("qn", [128, 512], F32))
            masks = ea(nc.sbuf_tensor("masks", [128, 4, 512], BF16))
            pat = ea(nc.sbuf_tensor("pat", [128, 6, 2, 512], BF16))
            sqscr = ea(nc.sbuf_tensor("sqscr", [128, 1024], BF16))

            ca_ps = ca.enter_context(ExitStack())
            qkv_ps = ca_ps.enter_context(nc.psum_tensor("qkv_ps", [128, 2, 1024], F32))
            tp_ps = ca_ps.enter_context(nc.psum_tensor("tp_ps", [128, 2, 1024], BF16))

            @block.gpsimd
            def _(g):
                dwA.inc(
                    g.dma_start(
                        out=wqkv_sb[:, :, :],
                        in_=wqkvT.ap().rearrange("(kc p) n -> p kc n", p=128),
                    ),
                    16,
                )
                dwA.inc(g.dma_start(out=bqkv_sb[:, :], in_=bqkvr[:, :]), 16)
                dwA.inc(g.dma_start(out=masks[:, :, :], in_=masksi[:, :, :]), 16)

            @block.sync
            def _(sy):
                dc1.inc(sy.dma_start(out=normw_sb[:, :], in_=normw[:, :]), 16)

            wq_tok = dwA.tok()
            consts_g = wq_tok
            nw_tok = dc1.tok()

            xs_tok = [None] * 16
            rope_tok = [None] * 16
            mm_tok = [None] * NT1
            evac_tok = [None] * NT1
            red_tok = [None] * NT1
            sd_tok = [None] * NT1
            rope_done = [None] * NT1
            vcp_tok = [None] * NT1
            tpmm_tok = [None] * NT1
            tp_tok = [None] * NT1

            def p1_transposes(tq):
                @block.tensor
                def _(te, tq=tq):
                    W.w(te, "t", rope_done[tq], consts_g)
                    if tq >= 2:
                        W.w(te, "t", tp_tok[tq - 2])
                    for hp in range(4):
                        tr = te.transpose(
                            tp_ps[:, tq % 2, hp * 128 : (hp + 1) * 128],
                            rqk[:, tq % 4, hp * 128 : (hp + 1) * 128],
                            idn[:, :],
                        )
                    tpmm_tok[tq] = tt.inc(tr)

                @block.scalar
                def _(s, tq=tq):
                    W.w(s, "s", tpmm_tok[tq])
                    for hp in range(2):
                        s.copy(
                            qf[:, hp, tq * 128 : (tq + 1) * 128],
                            tp_ps[:, tq % 2, hp * 128 : (hp + 1) * 128],
                        )
                        cp = s.copy(
                            kf[:, hp, tq * 128 : (tq + 1) * 128],
                            tp_ps[:, tq % 2, (2 + hp) * 128 : (3 + hp) * 128],
                        )
                    tp_tok[tq] = ss.inc(cp)

            for t in range(NT1):
                sl, tl = t // 4, t % 4

                if tl == 0:

                    @block.gpsimd
                    def _(g, sl=sl):
                        W.w(g, "g", consts_g)
                        if sl >= 2:
                            W.w(g, "g", mm_tok[4 * (sl - 2) + 3])
                        xs_tok[sl] = dxs[sl % 2].inc(
                            g.dma_start(
                                out=xs[:, sl % 2, :, :],
                                in_=xfm[:, sl * 512 : (sl + 1) * 512].rearrange(
                                    "(kc p) j -> p kc j", p=128
                                ),
                            ),
                            16,
                        )

                    @block.sync
                    def _(sy, sl=sl):
                        if sl >= 2:
                            W.w(sy, "y", rope_done[4 * (sl - 2) + 3])
                        dro[sl % 2].inc(
                            sy.dma_start(
                                out=ropq_sb[:, sl % 2, :, :],
                                in_=ropeq[sl * 512 : (sl + 1) * 512, :].rearrange(
                                    "(t p) c -> p t c", p=128
                                ),
                            ),
                            16,
                        )
                        rope_tok[sl] = dro[sl % 2].inc(
                            sy.dma_start(
                                out=ropk_sb[:, sl % 2, :, :],
                                in_=ropek[sl * 512 : (sl + 1) * 512, :].rearrange(
                                    "(t p) c -> p t c", p=128
                                ),
                            ),
                            16,
                        )

                @block.tensor
                def _(te, t=t, sl=sl, tl=tl):
                    W.w(te, "t", xs_tok[sl], wq_tok)
                    if t >= 2:
                        W.w(te, "t", evac_tok[t - 2])
                    pb = t % 2
                    for kc in range(8):
                        te.matmul(
                            qkv_ps[:, pb, 0:512],
                            xs[:, sl % 2, kc, tl * 128 : (tl + 1) * 128],
                            wqkv_sb[:, kc, 0:512],
                            start=(kc == 0),
                            stop=False,
                        )
                        te.matmul(
                            qkv_ps[:, pb, 512:768],
                            xs[:, sl % 2, kc, tl * 128 : (tl + 1) * 128],
                            wqkv_sb[:, kc, 512:768],
                            start=(kc == 0),
                            stop=False,
                        )
                    te.matmul(
                        qkv_ps[:, pb, 0:512], ones1[:, :], bqkv_sb[:, 0:512],
                        start=False, stop=True,
                    )
                    mm_tok[t] = tt.inc(
                        te.matmul(
                            qkv_ps[:, pb, 512:768], ones1[:, :], bqkv_sb[:, 512:768],
                            start=False, stop=True,
                        )
                    )

                @block.scalar
                def _(s, t=t):
                    W.w(s, "s", mm_tok[t])
                    if t >= 2:
                        W.w(s, "s", rope_done[t - 2], vcp_tok[t - 2])
                    evac_tok[t] = ss.inc(
                        s.copy(qkvt[:, t % 2, :], qkv_ps[:, t % 2, 0:768])
                    )

                @block.vector
                def _(v, t=t):
                    W.w(v, "v", evac_tok[t])
                    if t >= 1:
                        W.w(v, "v", red_tok[t - 1])
                    sqv = vv.inc(
                        v.tensor_tensor(
                            sqb[:, :], qkvt[:, t % 2, 0:512], qkvt[:, t % 2, 0:512],
                            ALU.mult,
                        )
                    )
                    W.w(v, "v", sqv)
                    rs1 = vv.inc(
                        v.reduce_sum(
                            ssum[:, t % 2, :].unsqueeze(2),
                            sqb[:, :].rearrange("p (g d) -> p g d", d=64),
                            axis=AX.X,
                        )
                    )
                    W.w(v, "v", rs1)
                    red_tok[t] = vv.inc(
                        v.tensor_scalar(
                            ssum[:, t % 2, :], ssum[:, t % 2, :], 1.0 / 64, EPS_RMS,
                            ALU.mult, ALU.add,
                        )
                    )

                if t % 2 == 0:
                    continue

                @block.scalar
                def _(s, t=t):
                    for u in (t - 1, t):
                        W.w(s, "s", red_tok[u])
                        sd_tok[u] = ss.inc(
                            s.activation(rsb[:, u % 2, :], ssum[:, u % 2, :], ACT.Sqrt)
                        )

                for u in (t - 1, t):
                    sl, tl = u // 4, u % 4

                    @block.vector
                    def _(v, t=u, sl=sl, tl=tl):
                        W.w(v, "v", sd_tok[t], nw_tok)
                        r1 = vv.inc(v.reciprocal(rsb[:, t % 2, :], rsb[:, t % 2, :]))
                        W.w(v, "v", r1)
                        r2 = vv.inc(
                            v.tensor_tensor(
                                qn[:, :].rearrange("p (g d) -> p g d", d=64),
                                qkvt[:, t % 2, 0:512].rearrange("p (g d) -> p g d", d=64),
                                rsb[:, t % 2, :].unsqueeze(2).broadcast_to([128, 8, 64]),
                                ALU.mult,
                            )
                        )
                        W.w(v, "v", r2)
                        r3 = vv.inc(
                            v.tensor_tensor(qn[:, :], qn[:, :], normw_sb[:, :], ALU.mult)
                        )
                        W.w(v, "v", r3, rope_tok[sl])
                        if t >= 4 and tpmm_tok[t - 4] is not None:
                            W.w(v, "v", tpmm_tok[t - 4])
                        qv = qn[:, :].rearrange("p (g d) -> p g d", d=64)
                        ov = rqk[:, t % 4, :].rearrange("p (g d) -> p g d", d=64)
                        rv = rscr[:, :].rearrange("p (g d) -> p g d", d=32)
                        for part, rp in ((0, ropq_sb), (1, ropk_sb)):
                            sg = slice(4 * part, 4 * part + 4)
                            cE = rp[:, sl % 2, tl, 0:32].unsqueeze(1).broadcast_to(
                                [128, 4, 32]
                            )
                            sE = rp[:, sl % 2, tl, 32:64].unsqueeze(1).broadcast_to(
                                [128, 4, 32]
                            )
                            cO = rp[:, sl % 2, tl, 64:96].unsqueeze(1).broadcast_to(
                                [128, 4, 32]
                            )
                            sO = rp[:, sl % 2, tl, 96:128].unsqueeze(1).broadcast_to(
                                [128, 4, 32]
                            )
                            qe_ = qv[:, sg, 0:32]
                            qo_ = qv[:, sg, 32:64]
                            v.tensor_tensor(rv[:, 0:4, :], qe_, cE, ALU.mult)
                            a2 = vv.inc(v.tensor_tensor(rv[:, 4:8, :], qo_, sE, ALU.mult))
                            W.w(v, "v", a2)
                            vv.inc(
                                v.tensor_tensor(
                                    ov[:, sg, 0:32], rv[:, 0:4, :], rv[:, 4:8, :],
                                    ALU.subtract,
                                )
                            )
                            v.tensor_tensor(rv[:, 8:12, :], qo_, cO, ALU.mult)
                            a4 = vv.inc(v.tensor_tensor(rv[:, 12:16, :], qe_, sO, ALU.mult))
                            W.w(v, "v", a4)
                            rope_done[t] = vv.inc(
                                v.tensor_tensor(
                                    ov[:, sg, 32:64], rv[:, 8:12, :], rv[:, 12:16, :],
                                    ALU.add,
                                )
                            )
                        W.w(v, "v", evac_tok[t])
                        vcp_tok[t] = vv.inc(
                            v.tensor_copy(
                                vt[:, :, t, :],
                                qkvt[:, t % 2, 512:768].rearrange("p (u d) -> p u d", d=64),
                            )
                        )

                if t >= 3:
                    p1_transposes(t - 3)
                    p1_transposes(t - 2)

            p1_transposes(NT1 - 2)
            p1_transposes(NT1 - 1)

            p1_s = ss.n
            p1_v = vv.n

            ca_ps.close()
            z_ps = ea(nc.psum_tensor("z_ps", [128, 2, 512], F32))
            sc_ps = ea(nc.psum_tensor("sc_ps", [128, 3, 2, 512], F32))

            p2sched = [(qi, kt) for qi in range(NQ) for kt in range(4 * qi + 4)]
            NI = len(p2sched)
            sq_eng = []
            load2 = {0: 0.0, 1: 0.0}
            for hb in range(2):
                for qi, kt in p2sched:
                    diag = kt >= 4 * qi
                    cs_ = (172 + 1024) / 1.2
                    cv_ = ((120 + 1024 if diag else 120 + 512) + 58 + 512) / 0.96
                    en = 1 if diag else (0 if load2[0] + cs_ <= load2[1] + cv_ else 1)
                    sq_eng.append(en)
                    load2[en] += cs_ if en == 0 else cv_
            sqrel = [None] * (2 * NI)
            patrdy = [None] * (2 * NI)
            cnt2 = {0: 0, 1: 0}
            for g in range(2 * NI):
                qi, kt = p2sched[g % NI]
                en = sq_eng[g]
                cnt2[en] += 1
                sqrel[g] = (en, cnt2[en])
                if en == 1:
                    cnt2[en] += 1
                patrdy[g] = (en, cnt2[en])

            def p2tok(entry):
                en, n = entry
                return ((ss if en == 0 else vv), (p1_s if en == 0 else p1_v) + n)

            zev_tok = [None] * 32

            def emit_z2(te, g):
                qip, ktp = p2sched[g % NI]
                hb = g // NI
                W.w(te, "t", p2tok(patrdy[g]))
                if ktp == 0:
                    idx = hb * NQ + qip
                    if idx >= 1:
                        W.w(te, "t", (ze, idx))
                last = ktp == 4 * qip + 3
                for ub in range(2):
                    mm = te.matmul(
                        z_ps[ub * 64 : ub * 64 + 64, ub, :],
                        vt[:, 2 * hb + ub, ktp, :],
                        pat[:, g % 6, ub, :],
                        start=(ktp == 0),
                        stop=last,
                        tile_position=(0, ub * 64),
                    )
                if last:
                    tz.inc(mm)

            @block.tensor
            def _(te):
                W.w(te, "t", (ss, p1_s), (vv, p1_v))
                for g in range(2 * NI):
                    qi, kt = p2sched[g % NI]
                    hb = g // NI
                    slot = g % 3
                    if g >= 3:
                        W.w(te, "t", p2tok(sqrel[g - 3]))
                    for ub in range(2):
                        p0 = ub * 64
                        mm = te.matmul(
                            sc_ps[:, slot, ub, :],
                            kf[p0 : p0 + 64, hb, kt * 128 : (kt + 1) * 128],
                            qf[p0 : p0 + 64, hb, qi * 512 : (qi + 1) * 512],
                            start=True,
                            stop=True,
                        )
                    ts.inc(mm)
                    if g >= 5:
                        emit_z2(te, g - 5)
                for gtail in range(2 * NI - 5, 2 * NI):
                    emit_z2(te, gtail)

            def emit_squares(eng, ename, engine_id, my):
                for g in range(2 * NI + 4):
                    if engine_id == 0 and g >= 4:
                        gd = g - 4
                        qi_d, kt_d = p2sched[gd % NI]
                        hb_d = gd // NI
                        if kt_d == 4 * qi_d + 3:
                            idx = hb_d * NQ + qi_d
                            W.w(eng, ename, (tz, idx + 1))
                            eng.copy(
                                zsb[0:64, hb_d, qi_d * 512 : (qi_d + 1) * 512],
                                z_ps[0:64, 0, :],
                            )
                            zev_tok[idx] = ze.inc(
                                eng.copy(
                                    zsb[64:128, hb_d, qi_d * 512 : (qi_d + 1) * 512],
                                    z_ps[64:128, 1, :],
                                )
                            )
                    if g >= 2 * NI:
                        continue
                    qi, kt = p2sched[g % NI]
                    hb = g // NI
                    if sq_eng[g] != engine_id:
                        continue
                    W.w(eng, ename, (ts, g + 1))
                    slot = g % 3
                    if engine_id == 0:
                        my.inc(
                            eng.activation(
                                pat[:, g % 6, :, :], sc_ps[:, slot, :, :], ACT.Square
                            )
                        )
                    else:
                        W.w(eng, ename, my.tok())
                        if kt >= 4 * qi:
                            m0 = kt - 4 * qi
                            c1 = my.inc(
                                eng.tensor_tensor(
                                    sqscr[:, :].rearrange("p (u j) -> p u j", u=2),
                                    sc_ps[:, slot, :, :],
                                    masks[:, m0, :].unsqueeze(1).broadcast_to(
                                        [128, 2, 512]
                                    ),
                                    ALU.mult,
                                )
                            )
                        else:
                            c1 = my.inc(
                                eng.tensor_scalar(
                                    sqscr[:, :], sc_ps[:, slot, :, :], 1.0, None,
                                    ALU.mult,
                                )
                            )
                        W.w(eng, ename, c1)
                        my.inc(
                            eng.tensor_tensor(
                                pat[:, g % 6, :, :],
                                sqscr[:, :].rearrange("p (u j) -> p u j", u=2),
                                sqscr[:, :].rearrange("p (u j) -> p u j", u=2),
                                ALU.mult,
                            )
                        )

            @block.scalar
            def _(s):
                emit_squares(s, "s", 0, ss)

            @block.vector
            def _(v):
                emit_squares(v, "v", 1, vv)

            @block.gpsimd
            def _(g):
                for half in range(2):
                    zs_, za_ = (z_send1, z_all1) if half == 0 else (z_send2, z_all2)
                    for j in range(4):
                        W.w(g, "g", (ze, 16 * half + 4 * (j + 1)))
                        dzs.inc(
                            g.dma_start(
                                out=zs_[j, :, :, :].rearrange("u d s -> (u d) s"),
                                in_=zsb[:, half, 2048 * j : 2048 * (j + 1)],
                            ),
                            16,
                        )
                        W.w(g, "g", dzs.tok())
                        cs.inc(
                            g.collective_compute(
                                "AllGather",
                                ALU.bypass,
                                replica_groups=[list(range(NCORE))],
                                ins=[zs_[j, :, :, :].opt()],
                                outs=[za_[j, :, :, :, :].opt()],
                            )
                        )

        cc_tok = cs.tok()

        cdE = ctx.enter_context(ExitStack())
        hf = cdE.enter_context(nc.sbuf_tensor("hf", [128, 8, 2 * POWN], BF16))
        assign = cdE.enter_context(nc.sbuf_tensor("assign", [128, 16], F32))
        routT_sb = cdE.enter_context(nc.sbuf_tensor("routT_sb", [128, 8, 8], BF16))
        rb_sb = cdE.enter_context(nc.sbuf_tensor("rb_sb", [1, 8], BF16))
        lgsb = cdE.enter_context(nc.sbuf_tensor("lgsb", [128, 2, 8], F32))
        r8 = cdE.enter_context(nc.sbuf_tensor("r8", [128, 3, 8], F32))
        rst = cdE.enter_context(nc.sbuf_tensor("rst", [128, 6], F32))
        rt_ps = cdE.enter_context(nc.psum_tensor("rt_ps", [128, 512], F32))
        lnj1b_t = cdE.enter_context(nc.sbuf_tensor("lnj1b", [128, D], F32))
        lnj2b_t = cdE.enter_context(nc.sbuf_tensor("lnj2b", [128, D], F32))

        cbE = ExitStack()
        winT_sb = cbE.enter_context(nc.sbuf_tensor("winT_sb", [128, 8, 3072], BF16))
        binp_sb = cbE.enter_context(nc.sbuf_tensor("binp_sb", [1, 3072], BF16))
        x1fp = cbE.enter_context(nc.sbuf_tensor("x1fp", [128, 2, 8, 2, 128], BF16))
        qkvp = cbE.enter_context(nc.sbuf_tensor("qkvp", [128, 2, 2, 3072], BF16))
        qkvf_ps = cbE.enter_context(nc.psum_tensor("qkvf_ps", [128, 3, 512], F32))
        wch_tok = [None] * 6

        with ExitStack() as cb:
            eb = cb.enter_context
            zin = eb(nc.sbuf_tensor("zin", [128, 4, 2, 4, 512], BF16))
            lnj1 = eb(nc.sbuf_tensor("lnj1a", [128, D], F32))
            lnj2 = eb(nc.sbuf_tensor("lnj2a", [128, D], F32))
            woT_sb = eb(nc.sbuf_tensor("woT_sb", [128, 8, D], BF16))
            x0t = eb(nc.sbuf_tensor("x0t", [128, 2, D], F32))
            Lb = eb(nc.sbuf_tensor("Lb", [128, 2, D], F32))
            x1t = eb(nc.sbuf_tensor("x1t", [128, 2, D], F32))
            stats = eb(nc.sbuf_tensor("stats", [128, 2, 4], F32))
            ao_ps = eb(nc.psum_tensor("ao_ps", [128, 2, 1024], F32))

            zin_tok = [None] * 4
            x0_tok = [None] * NTO
            ao_tok = [None] * NTO
            L_tok = [None] * NTO
            x1_tok = [None] * NTO
            sp_tok = [None] * NTO

            @block.gpsimd
            def _(g):
                W.w(g, "g", (ze, 32), (dzs, 128))
                dwB.inc(
                    g.dma_start(
                        out=woT_sb[:, :, :],
                        in_=woT.ap().rearrange("(kc p) n -> p kc n", p=128),
                    ),
                    16,
                )

            wo_tok = dwB.tok()
            zr = cb.enter_context(nc.gpsimd.register("zr"))
            zb_r = cb.enter_context(nc.gpsimd.register("zb_r"))

            def p3_tail(t):
                x1_tok[t] = emit_ln(
                    Lb[:, t % 2, :], x1t[:, t % 2, :], stats, t % 2,
                    L_tok[t], None, None,
                )

                @block.sync
                def _(sy, t=t):
                    W.w(sy, "y", x1_tok[t])
                    sp_tok[t] = dsp[t % 2].inc(
                        sy.dma_start(out=x1_dram[t, :, :], in_=x1t[:, t % 2, :]), 16
                    )

                @block.gpsimd
                def _(g, t=t):
                    W.w(g, "g", x1_tok[t])
                    if t >= 1:
                        W.w(g, "g", (dxb, 16 * t))
                    dxb.inc(
                        g.dma_start(out=x1b_dram[t, :, :], in_=x1t[:, t % 2, :]), 16
                    )

                if 8 <= t < 14:

                    @block.gpsimd
                    def _(g, ch=t - 8):
                        if ch >= 1:
                            W.w(g, "g", (dwC, 16 * ch + 16))
                        wch_tok[ch] = dwC.inc(
                            g.dma_start(
                                out=winT_sb[:, :, ch * 512 : (ch + 1) * 512],
                                in_=winT.ap().rearrange("(kc p) n -> p kc n", p=128)[
                                    :, :, ch * 512 : (ch + 1) * 512
                                ],
                            ),
                            16,
                        )
                        if ch == 0:
                            wch_tok[0] = dwC.inc(
                                g.dma_start(out=binp_sb[:, :], in_=binpr[:, :]), 16
                            )

            for t in range(NTO):
                b, tl = t // 8, t % 8
                slg = 2 * b + tl // 4
                sl2 = slg % 2
                tj = tl % 4

                if t in (0, 4, 8):
                    slgs = (0, 1) if t == 0 else (t // 4 + 1,)
                    for slgl in slgs:

                        @block.gpsimd
                        def _(g, slgl=slgl):
                            W.w(g, "g", (ze, 32), (dzs, 128), dc1.tok())
                            if slgl >= 2:
                                W.w(g, "g", (dzi[slgl % 2], 32))
                            if slgl == 0:
                                g.reg_load(zr, zoff_sb[:1, :1])
                            bl, tll = slgl // 2, 4 * (slgl % 2)
                            for par, z_allX in ((0, z_all1), (1, z_all2)):
                                W.w(g, "g", (cs, 4 * (par + 1)))
                                base = bl * 4 * (2 * 64 * 2048) + (tll // 4) * 512
                                g.reg_mov(zb_r, base)
                                g.reg_add(zb_r, zb_r, zr)
                                zin_tok[slgl] = dzi[slgl % 2].inc(
                                    g.dma_start(
                                        out=zin[:, slgl, par, :, :],
                                        in_=bass.AP(
                                            z_allX,
                                            zb_r,
                                            [
                                                [64 * 2048, 2],
                                                [2048, 64],
                                                [2 * 64 * 2048, 4],
                                                [1, 512],
                                            ],
                                        ),
                                    ),
                                    16,
                                )

                @block.sync
                def _(sy, t=t, b=b, tl=tl):
                    W.w(sy, "y", (ze, 32), (dzs, 128))
                    if t >= 2:
                        W.w(sy, "y", L_tok[t - 2])
                    x0_tok[t] = dx0[t % 2].inc(
                        sy.dma_start(
                            out=x0t[:, t % 2, :], in_=x0[b, tl * 128 : (tl + 1) * 128, :]
                        ),
                        16,
                    )

                @block.tensor
                def _(te, t=t, sl2=sl2, tj=tj, slg=slg):
                    W.w(te, "t", zin_tok[slg], wo_tok)
                    if t >= 2:
                        W.w(te, "t", L_tok[t - 2])
                    pb = t % 2
                    for kc in range(8):
                        for nh in range(2):
                            mm = te.matmul(
                                ao_ps[:, pb, nh * 512 : (nh + 1) * 512],
                                zin[:, slg, kc % 2, kc // 2, tj * 128 : (tj + 1) * 128],
                                woT_sb[:, kc, nh * 512 : (nh + 1) * 512],
                                start=(kc == 0),
                                stop=(kc == 7),
                            )
                    ao_tok[t] = tt.inc(mm)

                @block.vector
                def _(v, t=t):
                    W.w(v, "v", ao_tok[t], x0_tok[t])
                    if t >= 2:
                        W.w(v, "v", x1_tok[t - 2], sp_tok[t - 2], dxb.tok())
                    L_tok[t] = vv.inc(
                        v.scalar_tensor_tensor(
                            Lb[:, t % 2, :],
                            x0t[:, t % 2, :],
                            2.0,
                            ao_ps[:, t % 2, 0:1024],
                            ALU.mult,
                            ALU.add,
                        )
                    )

                if t >= 1:
                    p3_tail(t - 1)

            p3_tail(NTO - 1)

            p3_v = vv.tok()
            p3_t = tt.tok()
            p3_s = ss.tok()
            p3_sp = [dsp[0].tok(), dsp[1].tok(), dxb.tok()]
            p3_x0 = [dx0[0].tok(), dx0[1].tok()]

        with ExitStack() as cd:
            ed = cd.enter_context
            lnj1 = lnj1b_t
            lnj2 = lnj2b_t
            h_tok = [None] * NTO
            hsp_tok = [None] * NTO
            hf_tok = [None] * NTO
            hfd_tok = [None] * NTO
            asn_tok = [None] * NTO
            lge_tok = [None] * NTO

            with ExitStack() as cc2:
                ec = cc2.enter_context
                woutT_sb = ec(nc.sbuf_tensor("woutT_sb", [128, 8, D], BF16))
                bout_sb = ec(nc.sbuf_tensor("bout_sb", [1, D], BF16))
                ln1g_sb = ec(nc.sbuf_tensor("ln1g_sb", [128, D], F32))
                ln1b_sb = ec(nc.sbuf_tensor("ln1b_sb", [128, D], F32))
                x1p = ec(nc.sbuf_tensor("x1p", [128, 2, 2, D], F32))
                sc4 = ec(nc.sbuf_tensor("sc4", [128, 4, 16], F32))
                at4 = ec(nc.sbuf_tensor("at4", [128, 2, 16], F32))
                mscr = ec(nc.sbuf_tensor("mscr", [128, D], F32))
                mctx = ec(nc.sbuf_tensor("mctx", [128, D], BF16))
                ctxf = ec(nc.sbuf_tensor("ctxf", [128, 8, 2, 128], BF16))
                Lb4 = ec(nc.sbuf_tensor("Lb4", [128, 2, D], F32))
                ht4 = ec(nc.sbuf_tensor("ht4", [128, 2, 2, D], F32))
                st4 = ec(nc.sbuf_tensor("st4", [128, 2, 4], F32))

                tpcb_ps = ec(nc.psum_tensor("tpcb_ps", [128, 2, 1024], BF16))
                ao2_ps = ec(nc.psum_tensor("ao2_ps", [128, 2, 512], F32))

                @block.gpsimd
                def _(g):
                    W.w(g, "g", cc_tok, p3_t, p3_s, p3_v, *p3_sp, *p3_x0)
                    W.w(g, "g", (dwC, 112))
                    dwC.inc(
                        g.dma_start(
                            out=woutT_sb[:, :, :],
                            in_=woutT.ap().rearrange("(kc p) n -> p kc n", p=128),
                        ),
                        16,
                    )
                    dwC.inc(g.dma_start(out=bout_sb[:, :], in_=boutr[:, :]), 16)

                win_tok = dwC.tok()

                @block.sync
                def _(sy):
                    W.w(sy, "y", p3_v, p3_s, p3_t)
                    dc4.inc(sy.dma_start(out=ln1g_sb[:, :], in_=ln1g[:, :]), 16)
                    dc4.inc(sy.dma_start(out=ln1b_sb[:, :], in_=ln1b[:, :]), 16)

                lng_tok = dc4.tok()

                @block.gpsimd
                def _(g):
                    W.w(g, "g", cc_tok, p3_t, p3_s, p3_v, *p3_sp, *p3_x0)
                    dwD.inc(
                        g.dma_start(
                            out=routT_sb[:, :, :],
                            in_=routT.ap().rearrange("(kc p) n -> p kc n", p=128),
                        ),
                        16,
                    )
                    dwD.inc(g.dma_start(out=rb_sb[:, :], in_=rbr[:, :]), 16)

                rt_tok = dwD.tok()

                x1p_tok = [None] * 8
                x1f_tok = [None] * 8
                qmm_last = [None] * 8
                tpcb_prev = [None]
                ctx_tok = [None] * 8
                ctxf_ev = [None] * 8
                ao2_tok = [[None, None] for _ in range(8)]
                L1_tok = [[None, None] for _ in range(8)]

                def p4_new_sync(pr):
                    tA, tB = pr, 8 + pr

                    @block.sync
                    def _(sy, pr=pr, tA=tA, tB=tB):
                        W.w(sy, "y", *p3_sp)
                        if pr == 0:
                            W.w(sy, "y", p3_v, p3_s, p3_t)
                        if pr >= 2:
                            W.w(sy, "y", L1_tok[pr - 2][1])
                        dxp[pr % 2].inc(
                            sy.dma_start(out=x1p[:, pr % 2, 0, :], in_=x1_dram[tA, :, :]),
                            16,
                        )
                        x1p_tok[pr] = dxp[pr % 2].inc(
                            sy.dma_start(out=x1p[:, pr % 2, 1, :], in_=x1_dram[tB, :, :]),
                            16,
                        )

                    @block.sync
                    def _(sy, pr=pr, tA=tA, tB=tB):
                        W.w(sy, "y", (dxb, 16 * (9 + pr)))
                        if pr == 0:
                            W.w(sy, "y", cc_tok)
                        if pr >= 2:
                            W.w(sy, "y", (tt, qmm_last[pr - 2]))
                        for tx, tg in ((0, tA), (1, tB)):
                            for kc in range(8):
                                x1f_tok[pr] = dxf[pr % 2].inc(
                                    sy.dma_start_transpose(
                                        x1fp[:, pr % 2, kc, tx, :],
                                        x1b_dram[tg, :, kc * 128 : (kc + 1) * 128],
                                    ),
                                    16,
                                )

                def p4_new_te(pr):
                    @block.tensor
                    def _(te, pr=pr):
                        W.w(te, "t", x1f_tok[pr])
                        for tx in range(2):
                            for ch in range(6):
                                ci = 6 * tx + ch
                                W.w(te, "t", wch_tok[ch], (qe, 12 * pr + ci - 2))
                                pb = ci % 3
                                for kc in range(8):
                                    te.matmul(
                                        qkvf_ps[:, pb, :],
                                        x1fp[:, pr % 2, kc, tx, :],
                                        winT_sb[:, kc, ch * 512 : (ch + 1) * 512],
                                        start=(kc == 0),
                                        stop=False,
                                    )
                                te.matmul(
                                    qkvf_ps[:, pb, :],
                                    ones1[:, :],
                                    binp_sb[:, ch * 512 : (ch + 1) * 512],
                                    start=False,
                                    stop=True,
                                ).then_inc(tt.h, 1)
                                tt.n += 1
                        qmm_last[pr] = tt.n

                    qmm_base = tt.n - 12

                    @block.scalar
                    def _(s, pr=pr, qmm_base=qmm_base):
                        if pr >= 2:
                            W.w(s, "s", ctx_tok[pr - 2])
                        for tx in range(2):
                            for ch in range(6):
                                ci = 6 * tx + ch
                                W.w(s, "s", (tt, qmm_base + ci + 1))
                                qe.inc(
                                    s.copy(
                                        qkvp[:, pr % 2, tx, ch * 512 : (ch + 1) * 512],
                                        qkvf_ps[:, ci % 3, :],
                                    )
                                )

                    qev_done = qe.tok()

                    @block.vector
                    def _(v, pr=pr, qev_done=qev_done):
                        W.w(v, "v", qev_done)
                        if pr >= 1:
                            W.w(v, "v", ctxf_ev[pr - 1])
                        qA = qkvp[:, pr % 2, 0, 0:1024].rearrange("p (g d) -> p g d", d=64)
                        kA = qkvp[:, pr % 2, 0, 1024:2048].rearrange("p (g d) -> p g d", d=64)
                        qB = qkvp[:, pr % 2, 1, 0:1024].rearrange("p (g d) -> p g d", d=64)
                        kB = qkvp[:, pr % 2, 1, 1024:2048].rearrange("p (g d) -> p g d", d=64)
                        mv = mscr[:, :].rearrange("p (g d) -> p g d", d=64)
                        for i, (qq, kk) in enumerate(
                            ((qA, kA), (qA, kB), (qB, kA), (qB, kB))
                        ):
                            W.w(v, "v", vv.tok())
                            c1 = vv.inc(v.tensor_tensor(mv, qq, kk, ALU.mult))
                            W.w(v, "v", c1)
                            vv.inc(
                                v.reduce_sum(sc4[:, i, :].unsqueeze(2), mv, axis=AX.X)
                            )
                        W.w(v, "v", vv.tok())
                        vv.inc(
                            v.tensor_tensor(
                                at4[:, 0, :], sc4[:, 0, :], sc4[:, 1, :], ALU.subtract
                            )
                        )
                        vv.inc(
                            v.tensor_tensor(
                                at4[:, 1, :], sc4[:, 2, :], sc4[:, 3, :], ALU.subtract
                            )
                        )

                    dif_tok = vv.tok()

                    @block.scalar
                    def _(s, pr=pr, dif_tok=dif_tok):
                        W.w(s, "s", dif_tok)
                        s.activation(sc4[:, 0, :], at4[:, 0, :], ACT.Sigmoid, scale=0.125)
                        s.activation(sc4[:, 1, :], at4[:, 0, :], ACT.Sigmoid, scale=-0.125)
                        s.activation(sc4[:, 2, :], at4[:, 1, :], ACT.Sigmoid, scale=0.125)
                        ss.inc(
                            s.activation(
                                sc4[:, 3, :], at4[:, 1, :], ACT.Sigmoid, scale=-0.125
                            )
                        )

                    sig_tok = ss.tok()

                    @block.vector
                    def _(v, pr=pr, sig_tok=sig_tok):
                        W.w(v, "v", sig_tok)
                        vA = qkvp[:, pr % 2, 0, 2048:3072].rearrange("p (g d) -> p g d", d=64)
                        vB = qkvp[:, pr % 2, 1, 2048:3072].rearrange("p (g d) -> p g d", d=64)
                        mcv = mctx[:, :].rearrange("p (g d) -> p g d", d=64)
                        for tx in range(2):
                            a_s = sc4[:, 2 * tx, :].unsqueeze(2).broadcast_to([128, 16, 64])
                            b_s = sc4[:, 2 * tx + 1, :].unsqueeze(2).broadcast_to([128, 16, 64])
                            cvv = qkvp[:, pr % 2, tx, 0:1024].rearrange(
                                "p (g d) -> p g d", d=64
                            )
                            W.w(v, "v", vv.tok())
                            c1 = vv.inc(v.tensor_tensor(mcv, vA, a_s, ALU.mult))
                            c2 = vv.inc(v.tensor_tensor(cvv, vB, b_s, ALU.mult))
                            W.w(v, "v", c2)
                            ctx_tok[pr] = vv.inc(v.tensor_tensor(cvv, cvv, mcv, ALU.add))

                def p4_mid(p):
                    tA, tB = p, 8 + p
                    for tx in range(2):
                        transpose8(
                            (lambda kc, p=p, tx=tx: qkvp[
                                :, p % 2, tx, kc * 128 : (kc + 1) * 128
                            ]),
                            (lambda kc, tx=tx: ctxf[:, kc, tx, :]),
                            tpcb_ps, idn, [ctx_tok[p]], tpcb_prev,
                        )
                    ctxf_ev[p] = ss.tok()

                    for tx, tg in ((0, tA), (1, tB)):

                        @block.tensor
                        def _(te, p=p, tx=tx):
                            W.w(te, "t", ctxf_ev[p], win_tok)
                            if tx == 1:
                                W.w(te, "t", L1_tok[p][0])
                            elif p >= 1:
                                W.w(te, "t", L1_tok[p - 1][1])
                            for nh in range(2):
                                for kc in range(8):
                                    te.matmul(
                                        ao2_ps[:, nh, :],
                                        ctxf[:, kc, tx, :],
                                        woutT_sb[:, kc, nh * 512 : (nh + 1) * 512],
                                        start=(kc == 0),
                                        stop=False,
                                    )
                                mm = te.matmul(
                                    ao2_ps[:, nh, :],
                                    ones1[:, :],
                                    bout_sb[:, nh * 512 : (nh + 1) * 512],
                                    start=False,
                                    stop=True,
                                )
                            ao2_tok[p][tx] = tt.inc(mm)

                        @block.vector
                        def _(v, p=p, tx=tx, tg=tg):
                            W.w(v, "v", ao2_tok[p][tx], x1p_tok[p])
                            if p >= 1:
                                W.w(v, "v", h_tok[tg - 1])
                            if p >= 2:
                                W.w(v, "v", dhb.tok(), dhs[0].tok(), dhs[1].tok())
                            L1_tok[p][tx] = vv.inc(
                                v.scalar_tensor_tensor(
                                    Lb4[:, tx, :],
                                    x1p[:, p % 2, tx, :],
                                    1.0,
                                    ao2_ps[:, 0:2, :],
                                    ALU.mult,
                                    ALU.add,
                                )
                            )

                        h_tok[tg] = emit_ln(
                            Lb4[:, tx, :], ht4[:, p % 2, tx, :], st4, tx,
                            L1_tok[p][tx], (ln1g_sb, ln1b_sb), [lng_tok],
                        )

                def p4_old(p):
                    for tx, tg in ((0, p), (1, 8 + p)):

                        @block.sync
                        def _(sy, p=p, tx=tx, tg=tg):
                            W.w(sy, "y", h_tok[tg])
                            hsp_tok[tg] = dhs[(2 * p + tx) % 2].inc(
                                sy.dma_start(
                                    out=h_dram[tg, :, :], in_=ht4[:, p % 2, tx, :]
                                ),
                                16,
                            )

                        @block.gpsimd
                        def _(g, p=p, tx=tx, tg=tg):
                            W.w(g, "g", h_tok[tg])
                            if p >= 1 and tx == 0:
                                W.w(g, "g", (dhb, 32 * p))
                            hf_tok[tg] = dhb.inc(
                                g.dma_start(
                                    out=hb_dram[tg, :, :], in_=ht4[:, p % 2, tx, :]
                                ),
                                16,
                            )

                    for tx, tg in ((0, p), (1, 8 + p)):

                        @block.sync
                        def _(sy, p=p, tx=tx, tg=tg):
                            W.w(sy, "y", hf_tok[8 + p])
                            if p >= 1:
                                W.w(sy, "y", (dhf[tx], 128 * p))
                            if p == 0 and tx == 0:
                                W.w(
                                    sy, "y", cc_tok, p3_t, p3_s, p3_v,
                                    *p3_sp, *p3_x0,
                                )
                            for kc in range(8):
                                hfd_tok[tg] = dhf[tx].inc(
                                    sy.dma_start_transpose(
                                        hf[:, kc, tg * 128 : (tg + 1) * 128],
                                        hb_dram[tg, :, kc * 128 : (kc + 1) * 128],
                                    ),
                                    16,
                                )

                def p4_rt(p):
                    for tx, tg in ((0, p), (1, 8 + p)):

                        @block.tensor
                        def _(te, p=p, tx=tx, tg=tg):
                            W.w(te, "t", hfd_tok[tg], rt_tok)
                            prev = 8 + p - 1 if tx == 0 else p
                            if p >= 1 or tx == 1:
                                W.w(te, "t", lge_tok[prev])
                            for kc in range(8):
                                te.matmul(
                                    rt_ps[:, 0:8],
                                    hf[:, kc, tg * 128 : (tg + 1) * 128],
                                    routT_sb[:, kc, :],
                                    start=(kc == 0),
                                    stop=False,
                                )
                            tt.inc(
                                te.matmul(
                                    rt_ps[:, 0:8], ones1[:, :], rb_sb[:, :],
                                    start=False, stop=True,
                                )
                            )

                        lg_tok = tt.tok()

                        @block.scalar
                        def _(s, p=p, tx=tx, tg=tg, lg_tok=lg_tok):
                            W.w(s, "s", lg_tok)
                            if p >= 1:
                                W.w(s, "s", asn_tok[tg - 1])
                            lge_tok[tg] = ss.inc(
                                s.copy(lgsb[:, tx, :], rt_ps[:, 0:8])
                            )

                        @block.vector
                        def _(v, tx=tx, tg=tg):
                            W.w(v, "v", lge_tok[tg], dc1.tok())
                            c1 = vv.inc(
                                v.reduce_max(rst[:, 0:1], lgsb[:, tx, :], axis=AX.X)
                            )
                            W.w(v, "v", c1)
                            c2 = vv.inc(
                                v.tensor_scalar(
                                    r8[:, 0, :], lgsb[:, tx, :], rst[:, 0:1], None,
                                    ALU.is_equal,
                                )
                            )
                            W.w(v, "v", c2)
                            c3 = vv.inc(
                                v.tensor_tensor(
                                    r8[:, 1, :], r8[:, 0, :], iot[:, :], ALU.mult
                                )
                            )
                            W.w(v, "v", c3)
                            c4 = vv.inc(
                                v.tensor_reduce(rst[:, 1:2], r8[:, 1, :], AX.X, ALU.min)
                            )
                            W.w(v, "v", c4)
                            c5 = vv.inc(
                                v.tensor_scalar(
                                    r8[:, 0, :], iot[:, :], rst[:, 1:2], None,
                                    ALU.is_equal,
                                )
                            )
                            W.w(v, "v", c5)
                            c6 = vv.inc(
                                v.scalar_tensor_tensor(
                                    r8[:, 2, :], r8[:, 0, :], -1e30, lgsb[:, tx, :],
                                    ALU.mult, ALU.add,
                                )
                            )
                            W.w(v, "v", c6)
                            c7 = vv.inc(v.reduce_max(rst[:, 2:3], r8[:, 2, :], axis=AX.X))
                            W.w(v, "v", c7)
                            c8 = vv.inc(
                                v.tensor_scalar(
                                    r8[:, 0, :], r8[:, 2, :], rst[:, 2:3], None,
                                    ALU.is_equal,
                                )
                            )
                            W.w(v, "v", c8)
                            c9 = vv.inc(
                                v.tensor_tensor(
                                    r8[:, 1, :], r8[:, 0, :], iot[:, :], ALU.mult
                                )
                            )
                            W.w(v, "v", c9)
                            c10 = vv.inc(
                                v.tensor_reduce(rst[:, 3:4], r8[:, 1, :], AX.X, ALU.min)
                            )
                            W.w(v, "v", c10)
                            c11 = vv.inc(
                                v.tensor_tensor(
                                    rst[:, 4:5], rst[:, 1:2], rst[:, 3:4], ALU.max
                                )
                            )
                            W.w(v, "v", c11)
                            asn_tok[tg] = vv.inc(
                                v.tensor_scalar(
                                    assign[:, tg : tg + 1], rst[:, 4:5], 1000.0, None,
                                    ALU.add,
                                )
                            )

                for pr in range(11):
                    if pr < 8:
                        p4_new_sync(pr)
                    if 2 <= pr <= 9:
                        p4_old(pr - 2)
                    if 1 <= pr <= 8:
                        p4_mid(pr - 1)
                    if pr < 8:
                        p4_new_te(pr)
                    if 3 <= pr <= 10:
                        p4_rt(pr - 3)

                p4_t = tt.tok()
                p4_s = ss.tok()
                p4_v = vv.tok()
                rtr_s = ss.tok()
                rtr_v = vv.tok()
                p4_hs = [dhs[0].tok(), dhs[1].tok(), dhb.tok()]
                p4_xp = [dxp[0].tok(), dxp[1].tok(), dxf[0].tok(), dxf[1].tok()]

            cbE.close()
            cdb = cd.enter_context(ExitStack())
            f_acc = cdb.enter_context(nc.sbuf_tensor("f_acc", [128, NTO, D], BF16))
            cd2 = cdb.enter_context(ExitStack())
            if True:
                ed2 = cd2.enter_context
                hef = ed2(nc.sbuf_tensor("hef", [128, 8, 2 * POWN], BF16))
                w1sb = ed2(nc.sbuf_tensor("w1sb", [128, 2, 8, D], BF16))
                w2sb = ed2(nc.sbuf_tensor("w2sb", [128, 2, 8, D], BF16))
                b1e = ed2(nc.sbuf_tensor("b1e", [128, 2, 8], F32))
                b2e = ed2(nc.sbuf_tensor("b2e", [1, 2, D], BF16))
                mcol = ed2(nc.sbuf_tensor("mcol", [128, 1], F32))
                he_ps = ed2(nc.psum_tensor("he_ps", [128, 2, 512], F32))
                oe_ps = ed2(nc.psum_tensor("oe_ps", [128, 2, 1024], F32))

                ln2g_sb = ed2(nc.sbuf_tensor("ln2g_sb", [128, D], BF16))
                ln2b_sb = ed2(nc.sbuf_tensor("ln2b_sb", [128, D], BF16))
                ht6 = ed2(nc.sbuf_tensor("ht6", [128, 2, D], F32))
                x1t6 = ed2(nc.sbuf_tensor("x1t6", [128, 2, D], F32))
                f2b = ed2(nc.sbuf_tensor("f2b", [128, 2, D], F32))
                f2o = ed2(nc.sbuf_tensor("f2o", [128, 2, D], BF16))
                st6 = ed2(nc.sbuf_tensor("st6", [128, 2, 4], F32))
                L3b = f2b
                obuf = ht6

                @block.gpsimd
                def _(g):
                    W.w(
                        g, "g", p4_t, p4_s, p4_v, *p4_hs, *p4_xp,
                        win_tok, lng_tok,
                    )
                    dc6.inc(g.dma_start(out=ln2g_sb[:, :], in_=ln2g[:, :]), 16)
                    dc6.inc(g.dma_start(out=ln2b_sb[:, :], in_=ln2b[:, :]), 16)

                ln2_tok = dc6.tok()

                h6_tok = [None] * NTO
                L2_tok = [None] * NTO
                f2_tok = [None] * NTO
                L3_tok = [None] * NTO
                o_tok = [None] * NTO
                od_tok = [None] * NTO

                def p6_tile(t):
                    @block.sync
                    def _(sy, t=t):
                        W.w(sy, "y", *p4_hs, *p3_sp)
                        if t >= 2:
                            W.w(sy, "y", L2_tok[t - 2], L3_tok[t - 2], od_tok[t - 2])
                        dp6[t % 2].inc(
                            sy.dma_start(out=ht6[:, t % 2, :], in_=h_dram[t, :, :]), 16
                        )
                        h6_tok[t] = dp6[t % 2].inc(
                            sy.dma_start(out=x1t6[:, t % 2, :], in_=x1_dram[t, :, :]),
                            16,
                        )

                    @block.vector
                    def _(v, t=t):
                        W.w(v, "v", h6_tok[t], (ff, 96 + t + 1), ln2_tok)
                        if t >= 2:
                            W.w(v, "v", o_tok[t - 2])
                        L2_tok[t] = vv.inc(
                            v.tensor_tensor(
                                f2b[:, t % 2, :], ht6[:, t % 2, :], f_acc[:, t, :],
                                ALU.add,
                            )
                        )

                    f2_tok[t] = emit_ln(
                        f2b[:, t % 2, :], f2o[:, t % 2, :], st6, t % 2,
                        L2_tok[t], (ln2g_sb, ln2b_sb), None,
                    )

                    @block.vector
                    def _(v, t=t):
                        W.w(v, "v", f2_tok[t])
                        if t >= 2:
                            W.w(v, "v", od_tok[t - 2])
                        L3_tok[t] = vv.inc(
                            v.tensor_tensor(
                                L3b[:, t % 2, :], x1t6[:, t % 2, :], f2o[:, t % 2, :],
                                ALU.add,
                            )
                        )

                    o_tok[t] = emit_ln(
                        L3b[:, t % 2, :], obuf[:, t % 2, :], st6, t % 2,
                        L3_tok[t], None, None,
                    )

                    @block.sync
                    def _(sy, t=t):
                        W.w(sy, "y", o_tok[t])
                        b, tl = t // 8, t % 8
                        od_tok[t] = dou[t % 2].inc(
                            sy.dma_start(
                                out=out[b, tl * 128 : (tl + 1) * 128, :],
                                in_=obuf[:, t % 2, :],
                            ),
                            16,
                        )

                wg_tok = [None] * 7
                wy_tok = [None] * 7
                oe_tok = [[None] * NTO for _ in range(7)]

                for ei in range(7):
                    e_ = ei + 1
                    pe = ei % 2

                    @block.gpsimd
                    def _(g, ei=ei, e_=e_, pe=pe):
                        if ei == 0:
                            W.w(
                                g, "g", p4_t, p4_s, p4_v, *p4_hs, *p4_xp,
                                win_tok, lng_tok,
                            )
                        if ei >= 2:
                            W.w(g, "g", oe_tok[ei - 2][NTO - 1])
                        dwe[pe].inc(
                            g.dma_start(
                                out=w1sb[:, pe, :, :],
                                in_=w1T[e_, :, :].rearrange("(kc p) n -> p kc n", p=128),
                            ),
                            16,
                        )
                        dwe[pe].inc(
                            g.dma_start(
                                out=w2sb[:, pe, :, :],
                                in_=w2T[e_, :, :].rearrange("(kc p) n -> p kc n", p=128),
                            ),
                            16,
                        )
                        wg_tok[ei] = dwe[pe].inc(
                            g.dma_start(out=b2e[:, pe, :], in_=b2r[e_ : e_ + 1, :]), 16
                        )

                    @block.sync
                    def _(sy, ei=ei, e_=e_, pe=pe):
                        if ei == 0:
                            W.w(
                                sy, "y", p4_t, p4_s, p4_v, *p4_hs, *p4_xp,
                                win_tok, lng_tok,
                            )
                        if ei >= 2:
                            W.w(sy, "y", (gg, 32 * (ei - 1)))
                        wy_tok[ei] = dbe[pe].inc(
                            sy.dma_start(out=b1e[:, pe, :], in_=b1c[e_, :, :]), 16
                        )

                    @block.tensor
                    def _(te, ei=ei, pe=pe):
                        W.w(te, "t", wg_tok[ei], wy_tok[ei])
                        if ei == 0:
                            W.w(te, "t", rtr_s, rtr_v)
                        for m in range(8):
                            for t4 in range(4):
                                ci = 4 * m + t4
                                W.w(te, "t", (gg, 32 * ei + ci - 1))
                                for kc in range(8):
                                    mm = te.matmul(
                                        he_ps[:, ci % 2, :],
                                        w1sb[:, pe, kc, m * 128 : (m + 1) * 128],
                                        hf[:, kc, t4 * 512 : (t4 + 1) * 512],
                                        start=(kc == 0),
                                        stop=(kc == 7),
                                    )
                                tt.inc(mm)

                    hmm_base = tt.n - 32

                    @block.scalar
                    def _(s, ei=ei, pe=pe, hmm_base=hmm_base):
                        for m in range(8):
                            for t4 in range(4):
                                ci = 4 * m + t4
                                W.w(s, "s", (tt, hmm_base + ci + 1))
                                if ei >= 1:
                                    W.w(s, "s", oe_tok[ei - 1][NTO - 1])
                                gg.inc(
                                    s.activation(
                                        hef[:, m, t4 * 512 : (t4 + 1) * 512],
                                        he_ps[:, ci % 2, :],
                                        ACT.Gelu,
                                        bias=b1e[:, pe, m : m + 1],
                                    )
                                )

                    @block.tensor
                    def _(te, ei=ei, pe=pe):
                        W.w(te, "t", (gg, 32 * (ei + 1)))
                        for t in range(NTO):
                            W.w(te, "t", (ff, 16 * ei + t - 1))
                            for nh in range(2):
                                for kc in range(8):
                                    te.matmul(
                                        oe_ps[:, t % 2, nh * 512 : (nh + 1) * 512],
                                        hef[:, kc, t * 128 : (t + 1) * 128],
                                        w2sb[:, pe, kc, nh * 512 : (nh + 1) * 512],
                                        start=(kc == 0),
                                        stop=False,
                                    )
                                mm = te.matmul(
                                    oe_ps[:, t % 2, nh * 512 : (nh + 1) * 512],
                                    ones1[:, :],
                                    b2e[:, pe, nh * 512 : (nh + 1) * 512],
                                    start=False,
                                    stop=True,
                                )
                            oe_tok[ei][t] = tt.inc(mm)

                    for t in range(NTO):

                        @block.vector
                        def _(v, ei=ei, e_=e_, t=t):
                            W.w(v, "v", oe_tok[ei][t], asn_tok[t], ff.tok())
                            mc = vv.inc(
                                v.tensor_single_scalar(
                                    mcol[:, 0:1], assign[:, t : t + 1], float(e_),
                                    ALU.is_equal,
                                )
                            )
                            W.w(v, "v", mc)
                            if ei == 0:
                                ff.inc(
                                    v.tensor_scalar(
                                        f_acc[:, t, :], oe_ps[:, t % 2, 0:1024],
                                        mcol[:, 0:1], None, ALU.mult,
                                    )
                                )
                            else:
                                ff.inc(
                                    v.scalar_tensor_tensor(
                                        f_acc[:, t, :],
                                        oe_ps[:, t % 2, 0:1024],
                                        mcol[:, 0:1],
                                        f_acc[:, t, :],
                                        ALU.mult,
                                        ALU.add,
                                    )
                                )

                        if ei == 6:
                            p6_tile(t)


                @block.gpsimd
                def _(g):
                    for t in range(NTO):
                        W.w(g, "g", od_tok[t])

    return nc


def _prep_inputs(inputs):
    x = np.ascontiguousarray(inputs["L0_attn_x"], dtype=np.float32)
    Wq = inputs["L0_attn_Wq"]
    bq = inputs["L0_attn_bq"]
    Wk = inputs["L0_attn_Wk"]
    bk = inputs["L0_attn_bk"]
    Wv = inputs["L0_attn_Wv"]
    bv = inputs["L0_attn_bv"]
    Wo = inputs["L0_attn_Wo"]
    norm_w = inputs["L0_attn_norm_w"]

    P64 = np.concatenate([np.arange(0, 64, 2), np.arange(1, 64, 2)])
    permfull = np.concatenate([h * 64 + P64 for h in range(16)])
    Wqp = Wq[permfull, :]
    Wkp = Wk[permfull, :]
    bqp = bq[permfull]
    bkp = bk[permfull]
    nwp = norm_w[P64]

    inv = 1.0 / (10000.0 ** (np.arange(0, 64, 2, dtype=np.float64) / 64.0))
    tpos = np.arange(S, dtype=np.float64)
    freqs = np.outer(tpos, inv)
    emb = np.concatenate([freqs, freqs], axis=-1)
    cos = np.cos(emb).astype(np.float32)
    sin = np.sin(emb).astype(np.float32)
    cE, sE = cos[:, 0::2], sin[:, 0::2]
    cO, sO = cos[:, 1::2], sin[:, 1::2]
    ropek = np.ascontiguousarray(
        np.concatenate([cE, sE, cO, sO], axis=1), dtype=np.float32
    )
    ropeq = (ropek / 64.0).astype(np.float32)

    masksv = np.zeros((128, 4, 512), np.float32)
    ar = np.arange(512)
    for m in range(4):
        for p in range(128):
            masksv[p, m, :] = (ar >= p + 128 * m).astype(np.float32)

    common = {
        "ropeq": ropeq,
        "ropek": ropek,
        "normw": np.ascontiguousarray(
            np.broadcast_to(np.tile(nwp, 8)[None, :], (128, 512)), dtype=np.float32
        ),
        "woT": np.ascontiguousarray(Wo.T, dtype=np.float32),
        "masksi": masksv,
        "identi": np.eye(128, dtype=np.float32),
        "winT": np.ascontiguousarray(inputs["L0_ffn_in_proj_w"].T, dtype=np.float32),
        "binpr": np.ascontiguousarray(
            inputs["L0_ffn_in_proj_b"][None, :], dtype=np.float32
        ),
        "woutT": np.ascontiguousarray(inputs["L0_ffn_out_w"].T, dtype=np.float32),
        "boutr": np.ascontiguousarray(inputs["L0_ffn_out_b"][None, :], dtype=np.float32),
        "ln1g": np.ascontiguousarray(
            np.broadcast_to(inputs["L0_ffn_ln1_g"][None, :], (128, D)), dtype=np.float32
        ),
        "ln1b": np.ascontiguousarray(
            np.broadcast_to(inputs["L0_ffn_ln1_b"][None, :], (128, D)), dtype=np.float32
        ),
        "ln2g": np.ascontiguousarray(
            np.broadcast_to(inputs["L0_ffn_ln2_g"][None, :], (128, D)), dtype=np.float32
        ),
        "ln2b": np.ascontiguousarray(
            np.broadcast_to(inputs["L0_ffn_ln2_b"][None, :], (128, D)), dtype=np.float32
        ),
        "routT": np.ascontiguousarray(inputs["L0_ffn_router_w"].T, dtype=np.float32),
        "rbr": np.ascontiguousarray(
            inputs["L0_ffn_router_b"][None, :], dtype=np.float32
        ),
        "iotam": np.ascontiguousarray(
            np.broadcast_to(
                (np.arange(8, dtype=np.float32) - 1000.0)[None, :], (128, 8)
            ),
            dtype=np.float32,
        ),
        "w1T": np.ascontiguousarray(
            inputs["L0_ffn_w1"].transpose(0, 2, 1), dtype=np.float32
        ),
        "b1c": np.ascontiguousarray(
            inputs["L0_ffn_b1"].reshape(8, 8, 128).transpose(0, 2, 1), dtype=np.float32
        ),
        "w2T": np.ascontiguousarray(
            inputs["L0_ffn_w2"].transpose(0, 2, 1), dtype=np.float32
        ),
        "b2r": np.ascontiguousarray(inputs["L0_ffn_b2"], dtype=np.float32),
    }

    in_maps = []
    for c in range(NCORE):
        bc = c // 4
        h0 = 4 * (c % 4) * 64
        sel = slice(h0, h0 + 256)
        m = dict(common)
        m["xfm"] = np.ascontiguousarray(x[bc].T, dtype=np.float32)
        m["x0"] = np.ascontiguousarray(
            x[:, c * POWN : (c + 1) * POWN, :], dtype=np.float32
        )
        m["wqkvT"] = np.ascontiguousarray(
            np.concatenate([Wqp[sel].T, Wkp[sel].T, Wv[sel].T], axis=1),
            dtype=np.float32,
        )
        m["bqkvr"] = np.ascontiguousarray(
            np.concatenate([bqp[sel], bkp[sel], bv[sel]])[None, :], dtype=np.float32
        )
        m["zoffi"] = np.array(
        [[(c // 2) * (NCORE * 2 * 64 * 2048) + (c % 2) * 1024]], dtype=np.uint32
    )
        in_maps.append(m)
    return in_maps


def kernel(**inputs):
    in_maps = _prep_inputs(inputs)
    if "nc" not in _CACHE:
        _CACHE["nc"] = build_nc()
    res = run_bass_kernel_spmd(_CACHE["nc"], in_maps, core_ids=list(range(NCORE)))
    return np.concatenate([r["out"] for r in res.results], axis=1)
